# version 13
# baseline (speedup 1.0000x reference)
import numpy as np
import ml_dtypes
import concourse.bass as bass
import concourse.mybir as mybir
from concourse.bass_utils import run_bass_kernel_spmd

F32 = mybir.dt.float32
BF16 = mybir.dt.bfloat16
I32 = mybir.dt.int32
AF = mybir.ActivationFunctionType
ALU = mybir.AluOpType
AXX = mybir.AxisListType.X

NCORES = 8
B, S, D = 2, 8192, 2048
NT = (B * S) // NCORES
TT = 128
NTT = NT // TT
GRP = 512
NG = NT // GRP
TPG = GRP // TT
C = D // 128
DQ = 3 * D
DF = 8192
ROT = 1024
EPS = 1e-5
QEPS = 1e-5
RSQRT8 = float(np.float32(1.0 / np.sqrt(np.float64(8.0))))

ENGINES = ("sync", "vector", "scalar", "tensor", "gpsimd")
SEMS = ENGINES + ("cc",)


class _TS:
    __slots__ = ("w", "r")

    def __init__(self):
        self.w = {}
        self.r = {}


class Prog:

    def __init__(self):
        self.ops = {e: [] for e in ENGINES}
        self.total = {e: 0 for e in SEMS}
        self.tiles = {}
        self.bar = {}
        self.chans = {}

    def _st(self, obj):
        k = id(obj)
        if k not in self.tiles:
            self.tiles[k] = _TS()
        return self.tiles[k]

    def barrier(self):
        self.bar = dict(self.total)

    def _chan(self, obj):
        k = id(obj)
        if k not in self.chans:
            self.chans[k] = "dq%d" % len(self.chans)
            self.total.setdefault(self.chans[k], 0)
        return self.chans[k]

    def emit(self, eng, fn, ins=(), outs=(), inc=1, is_async=False, deps=(),
             sem_eng=None, chan_obj=None):
        se = sem_eng or eng
        if is_async and sem_eng is None and chan_obj is not None:
            se = self._chan(chan_obj)
        need = {}
        self_wait = eng in ("vector", "scalar")

        def add(e, v, asy):
            if (e != eng or asy or self_wait) and v > need.get(e, 0):
                need[e] = v

        for e, v in self.bar.items():
            add(e, v, True)
        for t in ins:
            for e, (v, asy) in self._st(t).w.items():
                add(e, v, asy)
        for t in outs:
            st = self._st(t)
            for e, (v, asy) in st.w.items():
                add(e, v, asy)
            for e, (v, asy) in st.r.items():
                add(e, v, asy)
        for (e, v) in deps:
            add(e, v, True)
        if is_async and se == eng and self.total[eng] > 0:
            need[eng] = max(need.get(eng, 0), self.total[eng])
        drain = False
        if eng in ("vector", "scalar"):
            thr = self.total[eng] - 1
            for t in ins:
                w = self._st(t).w.get(eng)
                if w is not None and w[0] >= thr:
                    drain = True
            for t in outs:
                w = self._st(t).w.get(eng)
                if w is not None and w[0] >= thr:
                    drain = True
        self.total[se] += inc
        tok = (se, self.total[se])
        self.ops[eng].append((fn, need, inc, drain, se))
        for t in ins:
            self._st(t).r[se] = (tok[1], is_async)
        for t in outs:
            st = self._st(t)
            st.w[se] = (tok[1], is_async)
            st.r = {}
        return tok

    def build(self, nc, block, es):
        allsems = list(self.total.keys())
        sems = {e: es.enter_context(nc.semaphore("sem_" + e)) for e in allsems}
        waited = {e: set() for e in allsems}
        for e in ENGINES:
            for (fn, need, inc, drain, se) in self.ops[e]:
                for e2, v in need.items():
                    waited[e2].add(v)

        def mk(e):
            ops = self.ops[e]

            def emit_engine(eng_obj):
                seen = {x: 0 for x in allsems}
                for (fn, need, inc, drain, se) in ops:
                    for e2 in sorted(need):
                        v = need[e2]
                        if v > seen[e2]:
                            eng_obj.wait_ge(sems[e2], v)
                            seen[e2] = v
                    if drain:
                        eng_obj.drain()
                    fn(eng_obj).then_inc(sems[se], inc)
                if e == "gpsimd":
                    for s2 in allsems:
                        if self.total[s2] > 0 and (s2.startswith("dq")
                                                   or s2 == "gpsimd"):
                            eng_obj.wait_ge(sems[s2], self.total[s2])
            return emit_engine

        block.sync(mk("sync"))
        block.vector(mk("vector"))
        block.scalar(mk("scalar"))
        block.tensor(mk("tensor"))
        block.gpsimd(mk("gpsimd"))


def build_nc(dbg=(), upto=None):
    from contextlib import ExitStack
    nc = bass.Bass("TRN2", target_bir_lowering=False)

    def din(name, shape, dt=F32):
        return nc.dram_tensor(name, shape, dt, kind="ExternalInput").ap()

    x_in = din("x", [NT, D])
    cs_in = din("cs", [NT, 2 * ROT])
    eye_in = din("eye_w", [128, 128])
    bmask_in = din("bmask_w", [128, 2])
    bsel_in = din("bsel_w", [2, 1])

    Win = [din("w_in0", [D, DQ]), din("w_in1", [D, DQ])]
    Bin = [din("b_in0", [DQ]), din("b_in1", [DQ])]
    Wout = [din("w_out0", [D, D]), din("w_out1", [D, D])]
    Bout = [din("b_out0", [D]), din("b_out1", [D])]
    Gam = [din("gam0", [D]), din("gam1", [D])]
    Bet = [din("bet0", [D]), din("bet1", [D])]
    wg_in = din("wg", [DF, D])
    wu_in = din("wu", [DF, D])
    wd_in = din("wd", [D, DF])
    fac_in = din("facs", [8, D, D])
    fb_in = din("fbias", [D])

    y_out = nc.dram_tensor("y", [NT, D], F32, kind="ExternalOutput").ap()

    def scr(name, shape, dt=F32):
        kind = "ExternalOutput" if name in dbg else "Internal"
        return nc.dram_tensor(name, shape, dt, kind=kind).ap()

    xn_hbm = scr("xn_hbm", [NT, D])
    q_hbm = scr("q_hbm", [NT, D])
    s_hbm = scr("s_hbm", [NT, D])
    h_hbm = scr("h_hbm", [NT, DF])
    x1_hbm = scr("x1_hbm", [NT, D])
    wqgT_h = scr("wqgT", [D, DF], BF16)
    wquT_h = scr("wquT", [D, DF], BF16)
    wqdT_h = scr("wqdT", [DF, D], BF16)
    F_hbm = scr("F_hbm", [D, D])
    ccin = scr("ccin", [B, D])
    ccout = scr("ccout", [B, D])
    dbgt = scr("dbgt", [128, 4 * D])

    p = Prog()
    es = ExitStack()

    def sbt(stack, name, shape, dt=F32):
        return stack.enter_context(nc.sbuf_tensor(name, shape, dt))

    gam_rep = sbt(es, "gam_rep", [128, D])
    bet_rep = sbt(es, "bet_rep", [128, D])
    q_t = sbt(es, "q_t", [128, D])
    cs_t = sbt(es, "cs_t", [128, 2 * ROT])
    x_t = sbt(es, "x_t", [128, D])
    xn_t = sbt(es, "xn_t", [128, D])
    t1 = sbt(es, "t1", [128, D])
    t2 = sbt(es, "t2", [128, D])
    i32t = sbt(es, "i32t", [128, D // 2], I32)
    ev0 = sbt(es, "ev0", [128, 512])
    ev1 = sbt(es, "ev1", [128, 512])
    b_sl = sbt(es, "b_sl", [128, 512])
    wdt = sbt(es, "wdt", [128, 512], BF16)
    wdt2 = sbt(es, "wdt2", [128, 512], BF16)
    wqb = sbt(es, "wqb", [128, 128], BF16)
    eye = sbt(es, "eye", [128, 128])
    bmask = sbt(es, "bmask", [128, 2])
    bsel = sbt(es, "bsel", [2, 1])
    A_loc = sbt(es, "A_loc", [128, C])
    A_acc = sbt(es, "A_acc", [128, C])
    A2 = sbt(es, "A2", [128, 2 * C])
    ones_col = sbt(es, "ones_col", [128, 1])
    ones_row = sbt(es, "ones_row", [1, 128])
    st_mu = sbt(es, "st_mu", [128, 1])
    st_v = sbt(es, "st_v", [128, 1])
    st_r = sbt(es, "st_r", [128, 1])
    st_a = sbt(es, "st_a", [128, 1])
    st_b = sbt(es, "st_b", [128, 1])
    st_c = sbt(es, "st_c", [128, 1])
    pw_a = sbt(es, "pw_a", [128, 1])
    pw_b = sbt(es, "pw_b", [128, 1])
    sgl = sbt(es, "sgl", [1, 1])
    hm_all = sbt(es, "hm_all", [128, NTT])
    rs_g = sbt(es, "rs_g", [128, NTT])
    rs_u = sbt(es, "rs_u", [128, NTT])
    rs_d = sbt(es, "rs_d", [128, NTT])
    rs_o = sbt(es, "rs_o", [128, NTT])
    wsc = sbt(es, "wsc", [128, 4])
    arep = gam_rep
    rh_t = x_t
    gh = [object(), object()]
    g0m = [object() for _ in range(TPG)]

    pes = ExitStack()

    def pst(name, shape, dt=F32):
        return pes.enter_context(nc.psum_tensor(name, shape, dt))

    pm = [pst("pm%d" % i, [128, 512]) for i in range(8)]
    ptr_t = pm[6]
    ptr = ptr_t[:, 0:128]
    pA_t = pm[7]
    pA = pA_t[:, 0:C]
    pR_t = pm[5]
    pR = pR_t[:, 0:512]
    pS_t = pm[4]
    pS = pS_t[:, 0:4]

    E = p.emit

    def dma_in(dst_ap, src_ap, touched, src=(), eng="sync"):
        E(eng, lambda s, d=dst_ap, sr=src_ap: s.dma_start(out=d, in_=sr),
          ins=list(src), outs=list(touched), inc=16, is_async=True,
          chan_obj=touched[0])

    def dma_out(dst_ap, src_ap, read, wrote=(), eng="gpsimd"):
        E(eng, lambda s, d=dst_ap, sr=src_ap: s.dma_start(out=d, in_=sr),
          ins=list(read), outs=list(wrote), inc=16, is_async=True,
          chan_obj=read[0] if read else None)

    def bcast_in(tile, dst_ap, vec_ap, n):
        E("sync", lambda s, d=dst_ap, v=vec_ap, n=n: s.dma_start(
            out=d, in_=v[None, :].broadcast_to([128, n])),
          outs=[tile], inc=16, is_async=True, chan_obj=tile)

    def tblock(src_t, src_ap, dst_t, dst_ap, dst_marks=()):
        E("tensor", lambda t, sa=src_ap: t.transpose(ptr, sa, eye[:, :]),
          ins=[src_t, eye], outs=[ptr_t])
        E("vector", lambda v, da=dst_ap: v.tensor_copy(da, ptr),
          ins=[ptr_t], outs=list(dst_marks) or [dst_t])


    def red_wide(dst_st, dst_ap, src_t, ap_lo, ap_hi, kind="sum", absval=False):
        E("vector", lambda v: (v.reduce_sum if kind == "sum" else v.reduce_max)(
            dst_ap, ap_lo, axis=AXX, apply_absolute_value=absval),
          ins=[src_t], outs=[dst_st])
        E("vector", lambda v: (v.reduce_sum if kind == "sum" else v.reduce_max)(
            st_c[:, :], ap_hi, axis=AXX, apply_absolute_value=absval),
          ins=[src_t], outs=[st_c])
        if kind == "sum":
            E("vector", lambda v: v.tensor_add(dst_ap, dst_ap, st_c[:, :]),
              ins=[dst_st, st_c], outs=[dst_st])
        else:
            E("vector", lambda v: v.tensor_max(dst_ap, dst_ap, st_c[:, :]),
              ins=[dst_st, st_c], outs=[dst_st])

    def ln_stats(src_t, src_ap, w):
        assert w == D
        red_wide(st_mu, st_mu[:, :], src_t,
                 src_t[:, 0:1024], src_t[:, 1024:2048], "sum")
        E("vector", lambda v, w=w: v.tensor_scalar_mul(st_mu[:, :], st_mu[:, :],
                                                       1.0 / w),
          ins=[st_mu], outs=[st_mu])
        E("scalar", lambda s, a=src_ap, w=w: s.activation(out=t1[:, :w], in_=a,
                                                          func=AF.Square),
          ins=[src_t], outs=[t1])
        red_wide(st_v, st_v[:, :], t1, t1[:, 0:1024], t1[:, 1024:2048], "sum")
        E("vector", lambda v, w=w: v.tensor_scalar_mul(st_v[:, :], st_v[:, :],
                                                       1.0 / w),
          ins=[st_v], outs=[st_v])
        E("vector", lambda v: v.tensor_mul(st_a[:, :], st_mu[:, :], st_mu[:, :]),
          ins=[st_mu], outs=[st_a])
        E("vector", lambda v: v.tensor_sub(st_v[:, :], st_v[:, :], st_a[:, :]),
          ins=[st_v, st_a], outs=[st_v])
        E("vector", lambda v: v.tensor_scalar_add(st_v[:, :], st_v[:, :], EPS),
          ins=[st_v], outs=[st_v])
        E("scalar", lambda s: s.activation(out=st_r[:, :], in_=st_v[:, :],
                                           func=AF.Sqrt),
          ins=[st_v], outs=[st_r])
        E("vector", lambda v: v.reciprocal(st_r[:, :], st_r[:, :]),
          ins=[st_r], outs=[st_r])

    def round_clip(tile, ap_, w, lo, hi):
        E("vector", lambda v, a=ap_, w=w: v.tensor_copy(i32t[:, :w], a),
          ins=[tile], outs=[i32t])
        E("scalar", lambda s, w=w: s.activation(out=t1[:, :w], in_=i32t[:, :w],
                                                func=AF.Identity),
          ins=[i32t], outs=[t1])
        E("vector", lambda v, a=ap_, w=w: v.tensor_sub(a, a, t1[:, :w]),
          ins=[tile, t1], outs=[tile])
        E("scalar", lambda s, a=ap_, w=w: s.activation(out=xn_t[:, :w], in_=a,
                                                       func=AF.Sign),
          ins=[tile], outs=[xn_t])
        E("scalar", lambda s, a=ap_, w=w: s.activation(out=x_t[:, :w], in_=a,
                                                       func=AF.Abs),
          ins=[tile], outs=[x_t])
        E("vector", lambda v, w=w: v.tensor_single_scalar(x_t[:, :w], x_t[:, :w],
                                                          0.5, ALU.is_ge),
          ins=[x_t], outs=[x_t])
        E("vector", lambda v, w=w: v.tensor_mul(x_t[:, :w], x_t[:, :w],
                                                xn_t[:, :w]),
          ins=[x_t, xn_t], outs=[x_t])
        E("vector", lambda v, a=ap_, w=w: v.tensor_add(a, t1[:, :w], x_t[:, :w]),
          ins=[t1, x_t], outs=[tile])
        E("vector", lambda v, a=ap_, lo=lo, hi=hi: v.tensor_scalar(
            a, a, float(lo), float(hi), ALU.max, ALU.min),
          ins=[tile], outs=[tile])

    def rotary(tile, ap_base, row0):
        dma_in(cs_t[:, :], cs_in[row0:row0 + TT, :], [cs_t])
        left = lambda: tile[:, ap_base:ap_base + ROT]
        E("vector", lambda v: v.tensor_scalar_mul(
            rh_t[:, 0:ROT:2], tile[:, ap_base + 1:ap_base + ROT:2], -1.0),
          ins=[tile], outs=[rh_t])
        E("vector", lambda v: v.tensor_copy(rh_t[:, 1:ROT:2],
                                            tile[:, ap_base:ap_base + ROT:2]),
          ins=[tile], outs=[rh_t])
        E("vector", lambda v: v.tensor_mul(rh_t[:, :ROT], rh_t[:, :ROT],
                                           cs_t[:, ROT:]),
          ins=[rh_t, cs_t], outs=[rh_t])
        E("vector", lambda v: v.tensor_mul(left(), left(), cs_t[:, :ROT]),
          ins=[tile, cs_t], outs=[tile])
        E("vector", lambda v: v.tensor_add(left(), left(), rh_t[:, :ROT]),
          ins=[tile, rh_t], outs=[tile])

    E("vector", lambda v: v.memset(ones_col[:, :], 1.0), outs=[ones_col])
    E("vector", lambda v: v.memset(ones_row[:, :], 1.0), outs=[ones_row])
    dma_in(eye[:, :], eye_in, [eye])
    dma_in(bmask[:, :], bmask_in, [bmask])
    dma_in(bsel[:, :], bsel_in, [bsel])

    def phase_w():
        qa, qb_ = q_t[:, 0:1024], q_t[:, 1024:2048]
        for (w_ap, wqT_ap, RO, CO, wi) in ((wg_in, wqgT_h, DF, D, 0),
                                           (wu_in, wquT_h, DF, D, 1),
                                           (wd_in, wqdT_h, D, DF, 2)):
            E("vector", lambda v: v.memset(pw_a[:, :], 0.0), outs=[pw_a])
            for it in range(RO // 128):
                for cp in range(CO // 1024):
                    dma_in(qa, w_ap[it * 128:(it + 1) * 128,
                                    cp * 1024:(cp + 1) * 1024], [q_t])
                    E("vector", lambda v: v.reduce_sum(
                        pw_b[:, :], qa, axis=AXX, apply_absolute_value=True),
                      ins=[q_t], outs=[pw_b])
                    E("vector", lambda v: v.tensor_add(pw_a[:, :], pw_a[:, :],
                                                       pw_b[:, :]),
                      ins=[pw_a, pw_b], outs=[pw_a])
            E("tensor", lambda t: t.matmul(pS_t[:1, 0:1], pw_a[:, :],
                                           ones_col[:, :],
                                           start=True, stop=True),
              ins=[pw_a, ones_col], outs=[pS_t])
            E("vector", lambda v, n=RO * CO: v.tensor_scalar(
                sgl[:, :], pS_t[:1, 0:1], 1.0 / n, QEPS, ALU.mult, ALU.max),
              ins=[pS_t], outs=[sgl])
            E("tensor", lambda t: t.matmul(pS_t[:, 1:2], ones_row[:, :],
                                           sgl[:, :], start=True, stop=True),
              ins=[ones_row, sgl], outs=[pS_t])
            E("vector", lambda v, k=wi: v.tensor_copy(wsc[:, k:k + 1],
                                                      pS_t[:, 1:2]),
              ins=[pS_t], outs=[wsc])
            E("vector", lambda v, k=wi: v.tensor_scalar_mul(
                wsc[:, 3:4], wsc[:, k:k + 1], 0.5),
              ins=[wsc], outs=[wsc])
            for it in range(RO // 128):
                for cp in range(CO // 1024):
                    dma_in(qa, w_ap[it * 128:(it + 1) * 128,
                                    cp * 1024:(cp + 1) * 1024], [q_t])
                    E("scalar", lambda s: s.activation(out=qb_, in_=qa,
                                                       func=AF.Sign),
                      ins=[q_t], outs=[q_t])
                    E("scalar", lambda s: s.activation(out=qa, in_=qa,
                                                       func=AF.Abs),
                      ins=[q_t], outs=[q_t])
                    E("vector", lambda v: v.tensor_single_scalar(
                        qa, qa, wsc[:, 3:4], ALU.is_ge),
                      ins=[q_t, wsc], outs=[q_t])
                    E("vector", lambda v: v.tensor_mul(qa, qa, qb_),
                      ins=[q_t], outs=[q_t])
                    for ci in range(8):
                        E("tensor", lambda t, ci=ci: t.transpose(
                            ptr, q_t[:, ci * 128:(ci + 1) * 128], eye[:, :]),
                          ins=[q_t, eye], outs=[ptr_t])
                        E("vector", lambda v: v.tensor_copy(wqb[:, :], ptr),
                          ins=[ptr_t], outs=[wqb])
                        r0 = cp * 1024 + ci * 128
                        dma_out(wqT_ap[r0:r0 + 128, it * 128:(it + 1) * 128],
                                wqb[:, :], [wqb], wrote=[wqT_ap])

    for ci in range(C):
        dma_in(t2[:, :], fac_in[0, ci * 128:(ci + 1) * 128, :], [t2])
        for r in range(1, 8):
            dma_in(t1[:, :], fac_in[r, ci * 128:(ci + 1) * 128, :], [t1])
            E("vector", lambda v: v.tensor_add(t2[:, :], t2[:, :], t1[:, :]),
              ins=[t2, t1], outs=[t2])
        dma_out(F_hbm[ci * 128:(ci + 1) * 128, :], t2[:, :], [t2], wrote=[F_hbm])

    def attn_phase_A(li, xsrc, AR):
        G0, G1, G2, G3 = AR
        for g in range(NG):
            for tt in range(TPG):
                row0 = g * GRP + tt * TT
                dma_in(x_t[:, :], xsrc[row0:row0 + TT, :], [x_t], src=[xsrc])
                ln_stats(x_t, x_t[:, :], D)
                E("vector", lambda v: v.tensor_scalar(xn_t[:, :], x_t[:, :],
                                                      st_mu[:, :], st_r[:, :],
                                                      ALU.subtract, ALU.mult),
                  ins=[x_t, st_mu, st_r], outs=[xn_t])
                dma_out(xn_hbm[row0:row0 + TT, :], xn_t[:, :], [xn_t],
                        wrote=[xn_hbm])
                ln_stats(xn_t, xn_t[:, :], D)
                E("vector", lambda v: v.tensor_scalar(t2[:, :], xn_t[:, :],
                                                      st_mu[:, :], st_r[:, :],
                                                      ALU.subtract, ALU.mult),
                  ins=[xn_t, st_mu, st_r], outs=[t2])
                E("vector", lambda v: v.tensor_mul(t2[:, :], t2[:, :],
                                                   gam_rep[:, :]),
                  ins=[t2, gam_rep], outs=[t2])
                E("vector", lambda v: v.tensor_add(t2[:, :], t2[:, :],
                                                   bet_rep[:, :]),
                  ins=[t2, bet_rep], outs=[t2])
                for ci in range(C):
                    c0 = ci * 512 + tt * 128
                    tblock(t2, t2[:, ci * 128:(ci + 1) * 128], G0,
                           G0[:, c0:c0 + 128], dst_marks=[g0m[tt]])
            for op_i in range(DQ // 512):
                for h in range(2):
                    wsrc = Win[li][:, op_i * 512:(op_i + 1) * 512] \
                        .rearrange("(c q) o -> q c o", q=128)[:, h * 8:(h + 1) * 8, :]
                    E("sync", lambda s, h=h, w=wsrc: s.dma_start(
                        out=G3[:, h * 4096:(h + 1) * 4096]
                        .rearrange("q (c o) -> q c o", c=8), in_=w),
                      outs=[gh[h]], inc=16, is_async=True, chan_obj=gh[h])
                bcast_in(b_sl, b_sl[:, :], Bin[li][op_i * 512:(op_i + 1) * 512],
                         512)
                bs_ = (op_i % 2) * 4
                for ci in range(C):
                    for tt in range(TPG):
                        E("tensor", lambda t, ci=ci, tt=tt, bs_=bs_: t.matmul(
                            pm[bs_ + tt][:, :],
                            G0[:, ci * 512 + tt * 128: ci * 512 + tt * 128 + 128],
                            G3[:, ci * 512:(ci + 1) * 512],
                            start=(ci == 0), stop=(ci == C - 1)),
                          ins=[g0m[tt], gh[ci // 8]], outs=[pm[bs_ + tt]])
                o0 = op_i * 512
                for tt in range(TPG):
                    bank = pm[bs_ + tt]
                    row0 = g * GRP + tt * TT
                    if o0 < D:
                        E("vector", lambda v, bank=bank: v.scalar_tensor_tensor(
                            ev0[:, :], bank[:, :], 1.0, b_sl[:, :],
                            ALU.mult, ALU.add),
                          ins=[bank, b_sl], outs=[ev0])
                        dma_out(q_hbm[row0:row0 + TT, o0:o0 + 512], ev0[:, :],
                                [ev0], wrote=[q_hbm])
                    else:
                        if o0 < 2 * D:
                            c0 = tt * 2048 + (o0 - D)
                            dt_, da = G1, G1[:, c0:c0 + 512]
                        else:
                            c0 = tt * 2048 + (o0 - 2 * D)
                            dt_, da = G2, G2[:, c0:c0 + 512]
                        E("vector", lambda v, bank=bank, da=da:
                          v.scalar_tensor_tensor(
                              da, bank[:, :], 1.0, b_sl[:, :], ALU.mult,
                              ALU.add),
                          ins=[bank, b_sl], outs=[dt_])
            for tt in range(TPG):
                row0 = g * GRP + tt * TT
                if li == 1:
                    dma_in(q_t[:, :], q_hbm[row0:row0 + TT, :], [q_t],
                           src=[q_hbm])
                    rotary(q_t, 0, row0)
                    dma_out(q_hbm[row0:row0 + TT, :], q_t[:, :], [q_t],
                            wrote=[q_hbm])
                kap = G1[:, tt * 2048: tt * 2048 + D]
                vap = G2[:, tt * 2048: tt * 2048 + D]
                rotary(G1, tt * 2048, row0)
                if li == 0:
                    rotary(G2, tt * 2048, row0)
                E("scalar", lambda s, a=kap: s.activation(out=t1[:, :], in_=a,
                                                          func=AF.Square),
                  ins=[G1], outs=[t1])
                red_wide(st_a, st_a[:, :], t1, t1[:, 0:1024],
                         t1[:, 1024:2048], "sum")
                E("scalar", lambda s: s.activation(out=st_a[:, :], in_=st_a[:, :],
                                                   func=AF.Sqrt),
                  ins=[st_a], outs=[st_a])
                E("vector", lambda v: v.reciprocal(st_a[:, :], st_a[:, :]),
                  ins=[st_a], outs=[st_a])
                if li == 0:
                    E("scalar", lambda s, a=vap: s.activation(out=t1[:, :], in_=a,
                                                              func=AF.Square),
                      ins=[G2], outs=[t1])
                    red_wide(st_b, st_b[:, :], t1, t1[:, 0:1024],
                             t1[:, 1024:2048], "sum")
                    E("scalar", lambda s: s.activation(out=st_b[:, :],
                                                       in_=st_b[:, :],
                                                       func=AF.Sqrt),
                      ins=[st_b], outs=[st_b])
                    E("vector", lambda v: v.reciprocal(st_b[:, :], st_b[:, :]),
                      ins=[st_b], outs=[st_b])
                    E("vector", lambda v: v.tensor_mul(st_a[:, :], st_a[:, :],
                                                       st_b[:, :]),
                      ins=[st_a, st_b], outs=[st_a])
                E("vector", lambda v, ka=kap, va=vap: v.tensor_mul(t1[:, :], ka,
                                                                   va),
                  ins=[G1, G2], outs=[t1])
                E("vector", lambda v: v.tensor_scalar_mul(t1[:, :], t1[:, :],
                                                          st_a[:, :]),
                  ins=[t1, st_a], outs=[t1])
                first = (g == 0 and tt == 0)
                for ci in range(C):
                    E("tensor", lambda t, ci=ci: t.matmul(
                        pA[:, ci:ci + 1], t1[:, ci * 128:(ci + 1) * 128],
                        ones_col[:, :], start=True, stop=True),
                      ins=[t1, ones_col], outs=[pA_t])
                if first:
                    E("vector", lambda v: v.tensor_copy(A_acc[:, :], pA),
                      ins=[pA_t], outs=[A_acc])
                else:
                    E("vector", lambda v: v.tensor_add(A_acc[:, :], A_acc[:, :],
                                                       pA),
                      ins=[pA_t], outs=[A_acc])

    def exchange_A(li):
        E("vector", lambda v: v.tensor_copy(A_loc[:, :], A_acc[:, :]),
          ins=[A_acc], outs=[A_loc])
        E("vector", lambda v: v.tensor_scalar_mul(A2[:, 0:C], A_loc[:, :],
                                                  bmask[:, 0:1]),
          ins=[A_loc, bmask], outs=[A2])
        E("vector", lambda v: v.tensor_scalar_mul(A2[:, C:2 * C], A_loc[:, :],
                                                  bmask[:, 1:2]),
          ins=[A_loc, bmask], outs=[A2])
        cc_dst = ccin[:, :].rearrange("b (c q) -> q b c", c=C, q=128)
        a2_src = A2[:, :].rearrange("q (b c) -> q b c", b=B)
        E("gpsimd", lambda gp: gp.dma_start(out=cc_dst, in_=a2_src,
                                            allow_slow_non_contiguous=True),
          ins=[A2], outs=[ccin], inc=16, is_async=True)
        p.barrier()
        E("gpsimd", lambda gp: gp.collective_compute(
            "AllReduce", ALU.add, replica_groups=[list(range(NCORES))],
            ins=[ccin[:, :]], outs=[ccout[:, :]]),
          ins=[ccin], outs=[ccout], inc=1, is_async=True, sem_eng="cc")
        p.barrier()
        dma_in(t2[0:2, :], ccout[:, :], [t2], src=[ccout])
        for j in range(4):
            E("tensor", lambda t, j=j: t.matmul(
                pR[:1, :], bsel[:, :], t2[0:2, j * 512:(j + 1) * 512],
                start=True, stop=True),
              ins=[bsel, t2], outs=[pR_t])
            E("vector", lambda v, j=j: v.tensor_copy(
                t1[0:1, j * 512:(j + 1) * 512], pR[:1, :]),
              ins=[pR_t], outs=[t1])
        if li == 0:
            E("vector", lambda v: v.tensor_scalar(t1[0:1, :], t1[0:1, :],
                                                  0.5, 0.5, ALU.mult, ALU.add),
              ins=[t1], outs=[t1])
        for j in range(4):
            E("tensor", lambda t, j=j: t.matmul(
                pR[:, :], ones_row[:, :], t1[0:1, j * 512:(j + 1) * 512],
                start=True, stop=True),
              ins=[ones_row, t1], outs=[pR_t])
            E("vector", lambda v, j=j: v.tensor_copy(
                arep[:, j * 512:(j + 1) * 512], pR[:, :]),
              ins=[pR_t], outs=[arep])
        if dbg:
            dma_out(dbgt[:, 0:C], A_loc[:, :], [A_loc], wrote=[dbgt])
            dma_out(dbgt[:, D:2 * D], arep[:, :], [arep], wrote=[dbgt])
            dma_out(dbgt[0:2, 2 * D:3 * D], t2[0:2, :], [t2], wrote=[dbgt])
            dma_out(dbgt[0:1, 3 * D:4 * D], t1[0:1, :], [t1], wrote=[dbgt])
            E("gpsimd", lambda gp: gp.dma_start(out=dbgt[2:4, 3 * D:4 * D],
                                                in_=ccin[:, :]),
              ins=[ccin], outs=[dbgt], inc=16, is_async=True)
            dma_out(dbgt[4:6, 2 * D:2 * D + 2 * C], A2[0:2, :2 * C], [A2], wrote=[dbgt])

    def attn_phase_B(li, AR):
        G0, G1, G2, G3 = AR
        for g in range(NG):
            for tt in range(TPG):
                row0 = g * GRP + tt * TT
                dma_in(q_t[:, :], q_hbm[row0:row0 + TT, :], [q_t], src=[q_hbm])
                E("vector", lambda v: v.tensor_mul(q_t[:, :], q_t[:, :],
                                                   arep[:, :]),
                  ins=[q_t, arep], outs=[q_t])
                for ci in range(C):
                    c0 = ci * 512 + tt * 128
                    tblock(q_t, q_t[:, ci * 128:(ci + 1) * 128], G0,
                           G0[:, c0:c0 + 128], dst_marks=[g0m[tt]])
            for op_i in range(D // 512):
                for h in range(2):
                    wsrc = Wout[li][:, op_i * 512:(op_i + 1) * 512] \
                        .rearrange("(c q) o -> q c o", q=128)[:, h * 8:(h + 1) * 8, :]
                    E("sync", lambda s, h=h, w=wsrc: s.dma_start(
                        out=G3[:, h * 4096:(h + 1) * 4096]
                        .rearrange("q (c o) -> q c o", c=8), in_=w),
                      outs=[gh[h]], inc=16, is_async=True, chan_obj=gh[h])
                bcast_in(b_sl, b_sl[:, :], Bout[li][op_i * 512:(op_i + 1) * 512],
                         512)
                bs_ = (op_i % 2) * 4
                for ci in range(C):
                    for tt in range(TPG):
                        E("tensor", lambda t, ci=ci, tt=tt, bs_=bs_: t.matmul(
                            pm[bs_ + tt][:, :],
                            G0[:, ci * 512 + tt * 128: ci * 512 + tt * 128 + 128],
                            G3[:, ci * 512:(ci + 1) * 512],
                            start=(ci == 0), stop=(ci == C - 1)),
                          ins=[g0m[tt], gh[ci // 8]], outs=[pm[bs_ + tt]])
                for tt in range(TPG):
                    bank = pm[bs_ + tt]
                    row0 = g * GRP + tt * TT
                    osl = slice(op_i * 512, (op_i + 1) * 512)
                    dma_in(ev1[:, :], xn_hbm[row0:row0 + TT, osl], [ev1],
                           src=[xn_hbm])
                    E("vector", lambda v, bank=bank: v.scalar_tensor_tensor(
                        ev0[:, :], bank[:, :], 1.0, b_sl[:, :], ALU.mult,
                        ALU.add),
                      ins=[bank, b_sl], outs=[ev0])
                    E("vector", lambda v: v.tensor_add(ev0[:, :], ev0[:, :],
                                                       ev1[:, :]),
                      ins=[ev0, ev1], outs=[ev0])
                    dma_out(s_hbm[row0:row0 + TT, osl], ev0[:, :], [ev0],
                            wrote=[s_hbm])

    def ffn0_part1(kqT, wgu):
        kqm = [object() for _ in range(NTT)]
        E("vector", lambda v: v.memset(hm_all[:, :], 0.0), outs=[hm_all])
        for tt in range(NTT):
            row0 = tt * TT
            dma_in(t2[:, :], xn_hbm[row0:row0 + TT, :], [t2], src=[xn_hbm])
            red_wide(st_a, st_a[:, :], t2, t2[:, 0:1024], t2[:, 1024:2048],
                     "max", absval=True)
            E("vector", lambda v: v.tensor_scalar_max(st_a[:, :], st_a[:, :],
                                                      QEPS),
              ins=[st_a], outs=[st_a])
            E("vector", lambda v: v.reciprocal(st_r[:, :], st_a[:, :]),
              ins=[st_a], outs=[st_r])
            E("vector", lambda v: v.tensor_scalar(t2[:, :], t2[:, :],
                                                  st_r[:, :], 127.0,
                                                  ALU.mult, ALU.mult),
              ins=[t2, st_r], outs=[t2])
            round_clip(t2, t2[:, 0:D // 2], D // 2, -128.0, 127.0)
            round_clip(t2, t2[:, D // 2:D], D // 2, -128.0, 127.0)
            for ci in range(C):
                E("tensor", lambda t, ci=ci: t.transpose(
                    ptr, t2[:, ci * 128:(ci + 1) * 128], eye[:, :]),
                  ins=[t2, eye], outs=[ptr_t])
                c0 = ci * 2048 + tt * 128
                E("vector", lambda v, c0=c0: v.tensor_copy(kqT[:, c0:c0 + 128],
                                                           ptr),
                  ins=[ptr_t], outs=[kqm[tt]])
            E("vector", lambda v: v.tensor_scalar_mul(st_a[:, :], st_a[:, :],
                                                      1.0 / 127.0),
              ins=[st_a], outs=[st_a])
            E("vector", lambda v, tt=tt: v.tensor_tensor(rs_g[:, tt:tt + 1],
                                                         st_a[:, :],
                                                         wsc[:, 0:1], ALU.mult),
              ins=[st_a, wsc], outs=[rs_g])
            E("vector", lambda v, tt=tt: v.tensor_tensor(rs_u[:, tt:tt + 1],
                                                         st_a[:, :],
                                                         wsc[:, 1:2], ALU.mult),
              ins=[st_a, wsc], outs=[rs_u])
        for op_i in range(DF // 512):
            osl = slice(op_i * 512, (op_i + 1) * 512)
            E("sync", lambda s, osl=osl: s.dma_start(
                out=wgu[:, 0:8192].rearrange("q (c o) -> q c o", c=C),
                in_=wqgT_h[:, osl].rearrange("(c q) o -> q c o", q=128)),
              ins=[wqgT_h], outs=[wgu], inc=16, is_async=True, chan_obj=gh[0])
            E("sync", lambda s, osl=osl: s.dma_start(
                out=wgu[:, 8192:16384].rearrange("q (c o) -> q c o", c=C),
                in_=wquT_h[:, osl].rearrange("(c q) o -> q c o", q=128)),
              ins=[wquT_h], outs=[wgu], inc=16, is_async=True, chan_obj=gh[1])
            for tt in range(NTT):
                bg, bu = pm[2 * (tt % 4)], pm[2 * (tt % 4) + 1]
                for ci in range(C):
                    c0 = ci * 2048 + tt * 128
                    E("tensor", lambda t, ci=ci, c0=c0, bg=bg, tt=tt: t.matmul(
                        bg[:, :], kqT[:, c0:c0 + 128],
                        wgu[:, ci * 512:(ci + 1) * 512],
                        start=(ci == 0), stop=(ci == C - 1)),
                      ins=[kqm[tt], wgu], outs=[bg])
                    E("tensor", lambda t, ci=ci, c0=c0, bu=bu, tt=tt: t.matmul(
                        bu[:, :], kqT[:, c0:c0 + 128],
                        wgu[:, 8192 + ci * 512: 8192 + (ci + 1) * 512],
                        start=(ci == 0), stop=(ci == C - 1)),
                      ins=[kqm[tt], wgu], outs=[bu])
                E("scalar", lambda s, bg=bg, tt=tt: s.activation(
                    out=ev0[:, :], in_=bg[:, :], func=AF.Silu,
                    scale=rs_g[:, tt:tt + 1]),
                  ins=[bg, rs_g], outs=[ev0])
                E("vector", lambda v, bu=bu: v.tensor_mul(ev1[:, :], ev0[:, :],
                                                          bu[:, :]),
                  ins=[ev0, bu], outs=[ev1])
                E("vector", lambda v, tt=tt: v.tensor_scalar_mul(
                    ev1[:, :], ev1[:, :], rs_u[:, tt:tt + 1]),
                  ins=[ev1, rs_u], outs=[ev1])
                E("vector", lambda v: v.reduce_max(st_b[:, :], ev1[:, :],
                                                   axis=AXX,
                                                   apply_absolute_value=True),
                  ins=[ev1], outs=[st_b])
                E("vector", lambda v, tt=tt: v.tensor_max(hm_all[:, tt:tt + 1],
                                                          hm_all[:, tt:tt + 1],
                                                          st_b[:, :]),
                  ins=[st_b, hm_all], outs=[hm_all])
                dma_out(h_hbm[tt * TT:(tt + 1) * TT, osl], ev1[:, :], [ev1],
                        wrote=[h_hbm])
        for tt in range(NTT):
            E("vector", lambda v, tt=tt: v.tensor_scalar_max(
                st_a[:, :], hm_all[:, tt:tt + 1], QEPS),
              ins=[hm_all], outs=[st_a])
            E("vector", lambda v: v.reciprocal(st_r[:, :], st_a[:, :]),
              ins=[st_a], outs=[st_r])
            E("vector", lambda v, tt=tt: v.tensor_scalar_mul(
                rs_d[:, tt:tt + 1], st_r[:, :], 127.0),
              ins=[st_r], outs=[rs_d])
            E("vector", lambda v: v.tensor_scalar_mul(st_a[:, :], st_a[:, :],
                                                      1.0 / 127.0),
              ins=[st_a], outs=[st_a])
            E("vector", lambda v, tt=tt: v.tensor_tensor(rs_o[:, tt:tt + 1],
                                                         st_a[:, :],
                                                         wsc[:, 2:3], ALU.mult),
              ins=[st_a, wsc], outs=[rs_o])

    def ffn0_part2(k2T):
        k2m = [object() for _ in range(8)]

        def quant_tile(hf, t8):
                tt = hf * 8 + t8
                row0 = tt * TT
                for fp_i in range(DF // 512):
                    dma_in(ev0[:, :], h_hbm[row0:row0 + TT,
                                            fp_i * 512:(fp_i + 1) * 512], [ev0],
                           src=[h_hbm])
                    E("vector", lambda v, tt=tt: v.tensor_scalar_mul(
                        t2[:, :512], ev0[:, :], rs_d[:, tt:tt + 1]),
                      ins=[ev0, rs_d], outs=[t2])
                    round_clip(t2, t2[:, :512], 512, -128.0, 127.0)
                    for j in range(4):
                        fc = fp_i * 4 + j
                        c0 = fc * 1024 + t8 * 128
                        E("tensor", lambda t, j=j: t.transpose(
                            ptr, t2[:, j * 128:(j + 1) * 128], eye[:, :]),
                          ins=[t2, eye], outs=[ptr_t])
                        E("vector", lambda v, c0=c0, t8=t8: v.tensor_copy(
                            k2T[:, c0:c0 + 128], ptr),
                          ins=[ptr_t], outs=[k2m[t8]])

        def down_quad(hf, q4):
                for op_i in range(D // 512):
                    osl = slice(op_i * 512, (op_i + 1) * 512)
                    for fc in range(DF // 128):
                        wb = (wdt, wdt) if False else ((wdt, wdt2)[fc % 2],) * 1
                        wcur = (wdt, wdt2)[fc % 2]
                        dma_in(wcur[:, :], wqdT_h[fc * 128:(fc + 1) * 128, osl],
                               [wcur], src=[wqdT_h])
                        for k in range(4):
                            t8 = q4 * 4 + k
                            c0 = fc * 1024 + t8 * 128
                            bs_ = ((q4 * 4 + op_i) % 2) * 4
                            E("tensor", lambda t, c0=c0, k=k, fc=fc, wcur=wcur,
                              t8=t8, bs_=bs_: t.matmul(
                                  pm[bs_ + k][:, :], k2T[:, c0:c0 + 128],
                                  wcur[:, :],
                                  start=(fc == 0), stop=(fc == DF // 128 - 1)),
                              ins=[k2m[t8], wcur], outs=[pm[bs_ + k]])
                    for k in range(4):
                        t8 = q4 * 4 + k
                        tt = hf * 8 + t8
                        row0 = tt * TT
                        bs_ = ((q4 * 4 + op_i) % 2) * 4
                        E("scalar", lambda s, k=k, tt=tt, bs_=bs_: s.activation(
                            out=ev0[:, :], in_=pm[bs_ + k][:, :], func=AF.Silu,
                            scale=rs_o[:, tt:tt + 1]),
                          ins=[pm[bs_ + k], rs_o], outs=[ev0])
                        osl2 = slice(op_i * 512, (op_i + 1) * 512)
                        dma_in(ev1[:, :], s_hbm[row0:row0 + TT, osl2], [ev1],
                               src=[s_hbm])
                        dma_in(b_sl[:, :], x_in[row0:row0 + TT, osl2], [b_sl])
                        E("vector", lambda v: v.tensor_mul(ev1[:, :], ev1[:, :],
                                                           ev0[:, :]),
                          ins=[ev0, ev1], outs=[ev1])
                        E("vector", lambda v: v.tensor_add(ev1[:, :], ev1[:, :],
                                                           b_sl[:, :]),
                          ins=[ev1, b_sl], outs=[ev1])
                        dma_out(x1_hbm[row0:row0 + TT, osl2], ev1[:, :], [ev1],
                                wrote=[x1_hbm])

        for hf in range(2):
            for t8 in range(4):
                quant_tile(hf, t8)
            down_quad(hf, 0)
            for t8 in range(4, 8):
                quant_tile(hf, t8)
            down_quad(hf, 1)

    def ffn1(AR):
        G0, G1, G2, G3 = AR
        bcast_in(bet_rep, bet_rep[:, :], fb_in, D)
        for g in range(NG):
            for tt in range(TPG):
                row0 = g * GRP + tt * TT
                dma_in(xn_t[:, :], xn_hbm[row0:row0 + TT, :], [xn_t],
                       src=[xn_hbm])
                for ci in range(C):
                    c0 = ci * 512 + tt * 128
                    tblock(xn_t, xn_t[:, ci * 128:(ci + 1) * 128],
                           G0, G0[:, c0:c0 + 128], dst_marks=[g0m[tt]])
            for op_i in range(D // 512):
                osl = slice(op_i * 512, (op_i + 1) * 512)
                for h in range(2):
                    wsrc = F_hbm[:, osl] \
                        .rearrange("(c q) o -> q c o", q=128)[:, h * 8:(h + 1) * 8, :]
                    E("sync", lambda s, h=h, w=wsrc: s.dma_start(
                        out=G3[:, h * 4096:(h + 1) * 4096]
                        .rearrange("q (c o) -> q c o", c=8), in_=w),
                      ins=[F_hbm], outs=[gh[h]], inc=16, is_async=True,
                      chan_obj=gh[h])
                bs_ = (op_i % 2) * 4
                for ci in range(C):
                    for tt in range(TPG):
                        E("tensor", lambda t, ci=ci, tt=tt, bs_=bs_: t.matmul(
                            pm[bs_ + tt][:, :],
                            G0[:, ci * 512 + tt * 128: ci * 512 + tt * 128 + 128],
                            G3[:, ci * 512:(ci + 1) * 512],
                            start=(ci == 0), stop=(ci == C - 1)),
                          ins=[g0m[tt], gh[ci // 8]], outs=[pm[bs_ + tt]])
                for tt in range(TPG):
                    bank = pm[bs_ + tt]
                    row0 = g * GRP + tt * TT
                    E("vector", lambda v, bank=bank, osl=osl:
                      v.scalar_tensor_tensor(
                          ev0[:, :], bank[:, :], RSQRT8, bet_rep[:, osl],
                          ALU.mult, ALU.add),
                      ins=[bank, bet_rep], outs=[ev0])
                    E("scalar", lambda s: s.activation(out=ev0[:, :],
                                                       in_=ev0[:, :],
                                                       func=AF.Silu),
                      ins=[ev0], outs=[ev0])
                    dma_in(ev1[:, :], s_hbm[row0:row0 + TT, osl], [ev1],
                           src=[s_hbm])
                    dma_in(b_sl[:, :], x1_hbm[row0:row0 + TT, osl], [b_sl],
                           src=[x1_hbm])
                    E("vector", lambda v: v.tensor_mul(ev1[:, :], ev1[:, :],
                                                       ev0[:, :]),
                      ins=[ev0, ev1], outs=[ev1])
                    E("vector", lambda v: v.tensor_add(ev1[:, :], ev1[:, :],
                                                       b_sl[:, :]),
                      ins=[ev1, b_sl], outs=[ev1])
                    dma_out(y_out[row0:row0 + TT, osl], ev1[:, :], [ev1],
                            wrote=[y_out])

    def done(tag):
        return upto is not None and upto == tag

    ar_es = ExitStack()
    AR = tuple(sbt(ar_es, "AR%d" % i, [128, 8192]) for i in range(4))
    bcast_in(gam_rep, gam_rep[:, :], Gam[0], D)
    bcast_in(bet_rep, bet_rep[:, :], Bet[0], D)
    attn_phase_A(0, x_in, AR)
    phase_w()
    if not done("L0A"):
        if not done("noX"):
            exchange_A(0)
        if not done("L0X"):
            attn_phase_B(0, AR)
    p.barrier()
    ar_es.close()

    if not done("L0A") and not done("noX") and not done("L0X")             and not done("L0B"):
        ar_es = ExitStack()
        kqT = sbt(ar_es, "kqT", [128, 2 * 16384], BF16)
        wgu = sbt(ar_es, "wgu", [128, 16384], BF16)
        ffn0_part1(kqT, wgu)
        p.barrier()
        ar_es.close()

        if not done("F01"):
            ar_es = ExitStack()
            k2T = sbt(ar_es, "k2T", [128, 64 * 1024], BF16)
            ffn0_part2(k2T)
            p.barrier()
            ar_es.close()

            if not done("F02"):
                ar_es = ExitStack()
                AR = tuple(sbt(ar_es, "BR%d" % i, [128, 8192])
                           for i in range(4))
                bcast_in(gam_rep, gam_rep[:, :], Gam[1], D)
                bcast_in(bet_rep, bet_rep[:, :], Bet[1], D)
                attn_phase_A(1, x1_hbm, AR)
                exchange_A(1)
                attn_phase_B(1, AR)
                ffn1(AR)
                p.barrier()
                ar_es.close()

    with nc.Block() as block:
        p.build(nc, block, es)
    pes.close()
    es.close()
    return nc


def _rot_tables():
    try:
        import jax
        import jax.numpy as jnp
        cpu = jax.devices("cpu")[0]
        with jax.default_device(cpu):
            inv = 1.0 / 10000.0 ** (jnp.arange(0, ROT, 2, dtype=jnp.float32) / ROT)
            fr = jnp.arange(S, dtype=jnp.float32)[:, None] * inv[None, :]
            fr = jnp.repeat(fr, 2, axis=-1)
            cost = np.asarray(jnp.cos(fr), dtype=np.float32)
            sint = np.asarray(jnp.sin(fr), dtype=np.float32)
    except Exception:
        inv = (1.0 / 10000.0 ** (np.arange(0, ROT, 2, dtype=np.float32)
                                 / np.float32(ROT))).astype(np.float32)
        fr = (np.arange(S, dtype=np.float32)[:, None]
              * inv[None, :]).astype(np.float32)
        fr = np.repeat(fr, 2, axis=-1)
        cost, sint = np.cos(fr).astype(np.float32), np.sin(fr).astype(np.float32)
    return cost, sint


def kernel(L0_attn_X, L0_attn_W_in, L0_attn_b_in, L0_attn_W_out, L0_attn_b_out,
           L0_attn_gamma, L0_attn_beta, L0_ffn_w_gate, L0_ffn_w_up,
           L0_ffn_w_down, L1_attn_ln_w, L1_attn_ln_b, L1_attn_W_in,
           L1_attn_b_in, L1_attn_W_out, L1_attn_b_out, L1_ffn_factors,
           L1_ffn_bias):
    x = np.ascontiguousarray(np.asarray(L0_attn_X, dtype=np.float32))
    cost, sint = _rot_tables()
    cs_full = np.concatenate([cost, sint], axis=1)

    com = {
        "eye_w": np.eye(128, dtype=np.float32),
        "w_in0": np.ascontiguousarray(L0_attn_W_in, dtype=np.float32),
        "b_in0": np.ascontiguousarray(L0_attn_b_in, dtype=np.float32),
        "w_out0": np.ascontiguousarray(L0_attn_W_out, dtype=np.float32),
        "b_out0": np.ascontiguousarray(L0_attn_b_out, dtype=np.float32),
        "gam0": np.ascontiguousarray(L0_attn_gamma, dtype=np.float32),
        "bet0": np.ascontiguousarray(L0_attn_beta, dtype=np.float32),
        "wg": np.ascontiguousarray(L0_ffn_w_gate, dtype=np.float32),
        "wu": np.ascontiguousarray(L0_ffn_w_up, dtype=np.float32),
        "wd": np.ascontiguousarray(L0_ffn_w_down, dtype=np.float32),
        "gam1": np.ascontiguousarray(L1_attn_ln_w, dtype=np.float32),
        "bet1": np.ascontiguousarray(L1_attn_ln_b, dtype=np.float32),
        "w_in1": np.ascontiguousarray(L1_attn_W_in, dtype=np.float32),
        "b_in1": np.ascontiguousarray(L1_attn_b_in, dtype=np.float32),
        "w_out1": np.ascontiguousarray(L1_attn_W_out, dtype=np.float32),
        "b_out1": np.ascontiguousarray(L1_attn_b_out, dtype=np.float32),
        "facs": np.ascontiguousarray(L1_ffn_factors, dtype=np.float32),
        "fbias": np.ascontiguousarray(L1_ffn_bias, dtype=np.float32),
    }

    in_maps = []
    for c in range(NCORES):
        b = c // 4
        r0 = (c % 4) * NT
        bm = np.zeros((128, 2), np.float32)
        bm[:, b] = 1.0
        bs = np.zeros((2, 1), np.float32)
        bs[b, 0] = 1.0
        m = dict(com)
        m["x"] = np.ascontiguousarray(x[b, r0:r0 + NT, :])
        m["cs"] = np.ascontiguousarray(cs_full[r0:r0 + NT, :])
        m["bmask_w"] = bm
        m["bsel_w"] = bs
        in_maps.append(m)

    nc = build_nc()
    res = run_bass_kernel_spmd(nc, in_maps, core_ids=list(range(NCORES)))
    out = np.empty((B, S, D), np.float32)
    for c in range(NCORES):
        b = c // 4
        r0 = (c % 4) * NT
        out[b, r0:r0 + NT, :] = res.results[c]["y"]
    return out


if __name__ == "__main__":
    print("building program ...")
    nc = build_nc()
    n = sum(1 for _ in nc.all_instructions())
    print("instructions:", n)


# version 14 (final)
# speedup vs baseline: 1.0234x; 1.0234x over previous
import numpy as np
import ml_dtypes
import concourse.bass as bass
import concourse.mybir as mybir
from concourse.bass_utils import run_bass_kernel_spmd

F32 = mybir.dt.float32
BF16 = mybir.dt.bfloat16
I32 = mybir.dt.int32
AF = mybir.ActivationFunctionType
ALU = mybir.AluOpType
AXX = mybir.AxisListType.X

NCORES = 8
B, S, D = 2, 8192, 2048
NT = (B * S) // NCORES
TT = 128
NTT = NT // TT
GRP = 512
NG = NT // GRP
TPG = GRP // TT
C = D // 128
DQ = 3 * D
DF = 8192
ROT = 1024
EPS = 1e-5
QEPS = 1e-5
RSQRT8 = float(np.float32(1.0 / np.sqrt(np.float64(8.0))))

ENGINES = ("sync", "vector", "scalar", "tensor", "gpsimd")
SEMS = ENGINES + ("cc",)


class _TS:
    __slots__ = ("w", "r")

    def __init__(self):
        self.w = {}
        self.r = {}


class Prog:

    def __init__(self):
        self.ops = {e: [] for e in ENGINES}
        self.total = {e: 0 for e in SEMS}
        self.tiles = {}
        self.bar = {}
        self.chans = {}

    def _st(self, obj):
        k = id(obj)
        if k not in self.tiles:
            self.tiles[k] = _TS()
        return self.tiles[k]

    def barrier(self):
        self.bar = dict(self.total)

    def _chan(self, obj):
        k = id(obj)
        if k not in self.chans:
            self.chans[k] = "dq%d" % len(self.chans)
            self.total.setdefault(self.chans[k], 0)
        return self.chans[k]

    def emit(self, eng, fn, ins=(), outs=(), inc=1, is_async=False, deps=(),
             sem_eng=None, chan_obj=None):
        se = sem_eng or eng
        if is_async and sem_eng is None and chan_obj is not None:
            se = self._chan(chan_obj)
        need = {}
        self_wait = eng in ("vector", "scalar")

        def add(e, v, asy):
            if (e != eng or asy or self_wait) and v > need.get(e, 0):
                need[e] = v

        for e, v in self.bar.items():
            add(e, v, True)
        for t in ins:
            for e, (v, asy) in self._st(t).w.items():
                add(e, v, asy)
        for t in outs:
            st = self._st(t)
            for e, (v, asy) in st.w.items():
                add(e, v, asy)
            for e, (v, asy) in st.r.items():
                add(e, v, asy)
        for (e, v) in deps:
            add(e, v, True)
        if is_async and se == eng and self.total[eng] > 0:
            need[eng] = max(need.get(eng, 0), self.total[eng])
        drain = False
        if eng in ("vector", "scalar"):
            thr = self.total[eng] - 1
            for t in ins:
                w = self._st(t).w.get(eng)
                if w is not None and w[0] >= thr:
                    drain = True
            for t in outs:
                w = self._st(t).w.get(eng)
                if w is not None and w[0] >= thr:
                    drain = True
        self.total[se] += inc
        tok = (se, self.total[se])
        self.ops[eng].append((fn, need, inc, drain, se))
        for t in ins:
            self._st(t).r[se] = (tok[1], is_async)
        for t in outs:
            st = self._st(t)
            st.w[se] = (tok[1], is_async)
            st.r = {}
        return tok

    def build(self, nc, block, es):
        allsems = list(self.total.keys())
        sems = {e: es.enter_context(nc.semaphore("sem_" + e)) for e in allsems}
        waited = {e: set() for e in allsems}
        for e in ENGINES:
            for (fn, need, inc, drain, se) in self.ops[e]:
                for e2, v in need.items():
                    waited[e2].add(v)

        def mk(e):
            ops = self.ops[e]

            def emit_engine(eng_obj):
                seen = {x: 0 for x in allsems}
                for (fn, need, inc, drain, se) in ops:
                    for e2 in sorted(need):
                        v = need[e2]
                        if v > seen[e2]:
                            eng_obj.wait_ge(sems[e2], v)
                            seen[e2] = v
                    if drain:
                        eng_obj.drain()
                    fn(eng_obj).then_inc(sems[se], inc)
                if e == "gpsimd":
                    for s2 in allsems:
                        if self.total[s2] > 0 and (s2.startswith("dq")
                                                   or s2 == "gpsimd"):
                            eng_obj.wait_ge(sems[s2], self.total[s2])
            return emit_engine

        block.sync(mk("sync"))
        block.vector(mk("vector"))
        block.scalar(mk("scalar"))
        block.tensor(mk("tensor"))
        block.gpsimd(mk("gpsimd"))


def build_nc(dbg=(), upto=None):
    from contextlib import ExitStack
    nc = bass.Bass("TRN2", target_bir_lowering=False)

    def din(name, shape, dt=F32):
        return nc.dram_tensor(name, shape, dt, kind="ExternalInput").ap()

    x_in = din("x", [NT, D])
    cs_in = din("cs", [NT, 2 * ROT])
    eye_in = din("eye_w", [128, 128])
    bmask_in = din("bmask_w", [128, 2])
    bsel_in = din("bsel_w", [2, 1])

    Win = [din("w_in0", [D, DQ]), din("w_in1", [D, DQ])]
    Bin = [din("b_in0", [DQ]), din("b_in1", [DQ])]
    Wout = [din("w_out0", [D, D]), din("w_out1", [D, D])]
    Bout = [din("b_out0", [D]), din("b_out1", [D])]
    Gam = [din("gam0", [D]), din("gam1", [D])]
    Bet = [din("bet0", [D]), din("bet1", [D])]
    wg_in = din("wg", [DF, D])
    wu_in = din("wu", [DF, D])
    wd_in = din("wd", [D, DF])
    fac_in = din("facs", [8, D, D])
    fb_in = din("fbias", [D])

    y_out = nc.dram_tensor("y", [NT, D], F32, kind="ExternalOutput").ap()

    def scr(name, shape, dt=F32):
        kind = "ExternalOutput" if name in dbg else "Internal"
        return nc.dram_tensor(name, shape, dt, kind=kind).ap()

    xn_hbm = scr("xn_hbm", [NT, D])
    q_hbm = scr("q_hbm", [NT, D])
    s_hbm = scr("s_hbm", [NT, D])
    h_hbm = scr("h_hbm", [NT, DF])
    x1_hbm = scr("x1_hbm", [NT, D])
    wqgT_h = scr("wqgT", [D, DF], BF16)
    wquT_h = scr("wquT", [D, DF], BF16)
    wqdT_h = scr("wqdT", [DF, D], BF16)
    F_hbm = scr("F_hbm", [D, D])
    ccin = scr("ccin", [B, D])
    ccout = scr("ccout", [B, D])
    dbgt = scr("dbgt", [128, 4 * D])

    p = Prog()
    es = ExitStack()

    def sbt(stack, name, shape, dt=F32):
        return stack.enter_context(nc.sbuf_tensor(name, shape, dt))

    gam_rep = sbt(es, "gam_rep", [128, D])
    bet_rep = sbt(es, "bet_rep", [128, D])
    q_t = sbt(es, "q_t", [128, D])
    cs_t = sbt(es, "cs_t", [128, 2 * ROT])
    x_t = sbt(es, "x_t", [128, D])
    xn_t = sbt(es, "xn_t", [128, D])
    t1 = sbt(es, "t1", [128, D])
    t2 = sbt(es, "t2", [128, D])
    i32t = sbt(es, "i32t", [128, D // 2], I32)
    ev0 = sbt(es, "ev0", [128, 512])
    ev1 = sbt(es, "ev1", [128, 512])
    b_sl = sbt(es, "b_sl", [128, 512])
    wdt = sbt(es, "wdt", [128, 512], BF16)
    wdt2 = sbt(es, "wdt2", [128, 512], BF16)
    wqb = sbt(es, "wqb", [128, 128], BF16)
    eye = sbt(es, "eye", [128, 128])
    bmask = sbt(es, "bmask", [128, 2])
    bsel = sbt(es, "bsel", [2, 1])
    A_loc = sbt(es, "A_loc", [128, C])
    A_acc = sbt(es, "A_acc", [128, C])
    A2 = sbt(es, "A2", [128, 2 * C])
    ones_col = sbt(es, "ones_col", [128, 1])
    ones_row = sbt(es, "ones_row", [1, 128])
    st_mu = sbt(es, "st_mu", [128, 1])
    st_v = sbt(es, "st_v", [128, 1])
    st_r = sbt(es, "st_r", [128, 1])
    st_a = sbt(es, "st_a", [128, 1])
    st_b = sbt(es, "st_b", [128, 1])
    st_c = sbt(es, "st_c", [128, 1])
    sgl = sbt(es, "sgl", [1, 1])
    hm_all = sbt(es, "hm_all", [128, NTT])
    rs_g = sbt(es, "rs_g", [128, NTT])
    rs_u = sbt(es, "rs_u", [128, NTT])
    rs_d = sbt(es, "rs_d", [128, NTT])
    rs_o = sbt(es, "rs_o", [128, NTT])
    wsc = sbt(es, "wsc", [128, 4])
    arep = gam_rep
    rh_t = x_t
    gh = [object(), object()]
    g0m = [object() for _ in range(TPG)]

    pes = ExitStack()

    def pst(name, shape, dt=F32):
        return pes.enter_context(nc.psum_tensor(name, shape, dt))

    pm = [pst("pm%d" % i, [128, 512]) for i in range(8)]
    ptr_t = pm[6]
    ptr = ptr_t[:, 0:128]
    pA_t = pm[7]
    pA = pA_t[:, 0:C]
    pR_t = pm[5]
    pR = pR_t[:, 0:512]
    pS_t = pm[4]
    pS = pS_t[:, 0:4]

    E = p.emit

    def dma_in(dst_ap, src_ap, touched, src=(), eng="sync"):
        E(eng, lambda s, d=dst_ap, sr=src_ap: s.dma_start(out=d, in_=sr),
          ins=list(src), outs=list(touched), inc=16, is_async=True,
          chan_obj=touched[0])

    def dma_out(dst_ap, src_ap, read, wrote=(), eng="gpsimd"):
        E(eng, lambda s, d=dst_ap, sr=src_ap: s.dma_start(out=d, in_=sr),
          ins=list(read), outs=list(wrote), inc=16, is_async=True,
          chan_obj=read[0] if read else None)

    def bcast_in(tile, dst_ap, vec_ap, n):
        E("sync", lambda s, d=dst_ap, v=vec_ap, n=n: s.dma_start(
            out=d, in_=v[None, :].broadcast_to([128, n])),
          outs=[tile], inc=16, is_async=True, chan_obj=tile)

    def tblock(src_t, src_ap, dst_t, dst_ap, dst_marks=()):
        E("tensor", lambda t, sa=src_ap: t.transpose(ptr, sa, eye[:, :]),
          ins=[src_t, eye], outs=[ptr_t])
        E("vector", lambda v, da=dst_ap: v.tensor_copy(da, ptr),
          ins=[ptr_t], outs=list(dst_marks) or [dst_t])


    def red_wide(dst_st, dst_ap, src_t, ap_lo, ap_hi, kind="sum", absval=False):
        E("vector", lambda v: (v.reduce_sum if kind == "sum" else v.reduce_max)(
            dst_ap, ap_lo, axis=AXX, apply_absolute_value=absval),
          ins=[src_t], outs=[dst_st])
        E("vector", lambda v: (v.reduce_sum if kind == "sum" else v.reduce_max)(
            st_c[:, :], ap_hi, axis=AXX, apply_absolute_value=absval),
          ins=[src_t], outs=[st_c])
        if kind == "sum":
            E("vector", lambda v: v.tensor_add(dst_ap, dst_ap, st_c[:, :]),
              ins=[dst_st, st_c], outs=[dst_st])
        else:
            E("vector", lambda v: v.tensor_max(dst_ap, dst_ap, st_c[:, :]),
              ins=[dst_st, st_c], outs=[dst_st])

    def ln_stats(src_t, src_ap, w):
        assert w == D
        red_wide(st_mu, st_mu[:, :], src_t,
                 src_t[:, 0:1024], src_t[:, 1024:2048], "sum")
        E("vector", lambda v, w=w: v.tensor_scalar_mul(st_mu[:, :], st_mu[:, :],
                                                       1.0 / w),
          ins=[st_mu], outs=[st_mu])
        E("scalar", lambda s, a=src_ap, w=w: s.activation(out=t1[:, :w], in_=a,
                                                          func=AF.Square),
          ins=[src_t], outs=[t1])
        red_wide(st_v, st_v[:, :], t1, t1[:, 0:1024], t1[:, 1024:2048], "sum")
        E("vector", lambda v, w=w: v.tensor_scalar_mul(st_v[:, :], st_v[:, :],
                                                       1.0 / w),
          ins=[st_v], outs=[st_v])
        E("vector", lambda v: v.tensor_mul(st_a[:, :], st_mu[:, :], st_mu[:, :]),
          ins=[st_mu], outs=[st_a])
        E("vector", lambda v: v.tensor_sub(st_v[:, :], st_v[:, :], st_a[:, :]),
          ins=[st_v, st_a], outs=[st_v])
        E("vector", lambda v: v.tensor_scalar_add(st_v[:, :], st_v[:, :], EPS),
          ins=[st_v], outs=[st_v])
        E("scalar", lambda s: s.activation(out=st_r[:, :], in_=st_v[:, :],
                                           func=AF.Sqrt),
          ins=[st_v], outs=[st_r])
        E("vector", lambda v: v.reciprocal(st_r[:, :], st_r[:, :]),
          ins=[st_r], outs=[st_r])

    def round_clip(tile, ap_, w, lo, hi):
        E("vector", lambda v, a=ap_, w=w: v.tensor_copy(i32t[:, :w], a),
          ins=[tile], outs=[i32t])
        E("scalar", lambda s, w=w: s.activation(out=t1[:, :w], in_=i32t[:, :w],
                                                func=AF.Identity),
          ins=[i32t], outs=[t1])
        E("vector", lambda v, a=ap_, w=w: v.tensor_sub(a, a, t1[:, :w]),
          ins=[tile, t1], outs=[tile])
        E("scalar", lambda s, a=ap_, w=w: s.activation(out=xn_t[:, :w], in_=a,
                                                       func=AF.Sign),
          ins=[tile], outs=[xn_t])
        E("scalar", lambda s, a=ap_, w=w: s.activation(out=x_t[:, :w], in_=a,
                                                       func=AF.Abs),
          ins=[tile], outs=[x_t])
        E("vector", lambda v, w=w: v.tensor_single_scalar(x_t[:, :w], x_t[:, :w],
                                                          0.5, ALU.is_ge),
          ins=[x_t], outs=[x_t])
        E("vector", lambda v, w=w: v.tensor_mul(x_t[:, :w], x_t[:, :w],
                                                xn_t[:, :w]),
          ins=[x_t, xn_t], outs=[x_t])
        E("vector", lambda v, a=ap_, w=w: v.tensor_add(a, t1[:, :w], x_t[:, :w]),
          ins=[t1, x_t], outs=[tile])
        E("vector", lambda v, a=ap_, lo=lo, hi=hi: v.tensor_scalar(
            a, a, float(lo), float(hi), ALU.max, ALU.min),
          ins=[tile], outs=[tile])

    def rotary(tile, ap_base, row0):
        dma_in(cs_t[:, :], cs_in[row0:row0 + TT, :], [cs_t])
        left = lambda: tile[:, ap_base:ap_base + ROT]
        E("vector", lambda v: v.tensor_scalar_mul(
            rh_t[:, 0:ROT:2], tile[:, ap_base + 1:ap_base + ROT:2], -1.0),
          ins=[tile], outs=[rh_t])
        E("vector", lambda v: v.tensor_copy(rh_t[:, 1:ROT:2],
                                            tile[:, ap_base:ap_base + ROT:2]),
          ins=[tile], outs=[rh_t])
        E("vector", lambda v: v.tensor_mul(rh_t[:, :ROT], rh_t[:, :ROT],
                                           cs_t[:, ROT:]),
          ins=[rh_t, cs_t], outs=[rh_t])
        E("vector", lambda v: v.tensor_mul(left(), left(), cs_t[:, :ROT]),
          ins=[tile, cs_t], outs=[tile])
        E("vector", lambda v: v.tensor_add(left(), left(), rh_t[:, :ROT]),
          ins=[tile, rh_t], outs=[tile])

    E("vector", lambda v: v.memset(ones_col[:, :], 1.0), outs=[ones_col])
    E("vector", lambda v: v.memset(ones_row[:, :], 1.0), outs=[ones_row])
    dma_in(eye[:, :], eye_in, [eye])
    dma_in(bmask[:, :], bmask_in, [bmask])
    dma_in(bsel[:, :], bsel_in, [bsel])

    for (w_ap, wqT_ap, RO, CO, wi) in ((wg_in, wqgT_h, DF, D, 0),
                                       (wu_in, wquT_h, DF, D, 1),
                                       (wd_in, wqdT_h, D, DF, 2)):
        E("vector", lambda v: v.memset(st_a[:, :], 0.0), outs=[st_a])
        for it in range(RO // 128):
            for cp in range(CO // D):
                dma_in(t1[:, :], w_ap[it * 128:(it + 1) * 128, cp * D:(cp + 1) * D],
                       [t1])
                red_wide(st_b, st_b[:, :], t1, t1[:, 0:1024],
                         t1[:, 1024:2048], "sum", absval=True)
                E("vector", lambda v: v.tensor_add(st_a[:, :], st_a[:, :],
                                                   st_b[:, :]),
                  ins=[st_a, st_b], outs=[st_a])
        E("tensor", lambda t: t.matmul(pS_t[:1, 0:1], st_a[:, :], ones_col[:, :],
                                       start=True, stop=True),
          ins=[st_a, ones_col], outs=[pS_t])
        E("vector", lambda v, n=RO * CO: v.tensor_scalar(
            sgl[:, :], pS_t[:1, 0:1], 1.0 / n, QEPS, ALU.mult, ALU.max),
          ins=[pS_t], outs=[sgl])
        E("tensor", lambda t: t.matmul(pS_t[:, 1:2], ones_row[:, :], sgl[:, :],
                                       start=True, stop=True),
          ins=[ones_row, sgl], outs=[pS_t])
        E("vector", lambda v, k=wi: v.tensor_copy(wsc[:, k:k + 1], pS_t[:, 1:2]),
          ins=[pS_t], outs=[wsc])
        E("vector", lambda v, k=wi: v.tensor_scalar_mul(wsc[:, 3:4],
                                                        wsc[:, k:k + 1], 0.5),
          ins=[wsc], outs=[wsc])
        for it in range(RO // 128):
            for cp in range(CO // D):
                dma_in(t1[:, :], w_ap[it * 128:(it + 1) * 128, cp * D:(cp + 1) * D],
                       [t1])
                E("scalar", lambda s: s.activation(out=t2[:, :], in_=t1[:, :],
                                                   func=AF.Sign),
                  ins=[t1], outs=[t2])
                E("scalar", lambda s: s.activation(out=xn_t[:, :], in_=t1[:, :],
                                                   func=AF.Abs),
                  ins=[t1], outs=[xn_t])
                E("vector", lambda v: v.tensor_single_scalar(t1[:, :], xn_t[:, :],
                                                             wsc[:, 3:4],
                                                             ALU.is_ge),
                  ins=[xn_t, wsc], outs=[t1])
                E("vector", lambda v: v.tensor_mul(t1[:, :], t1[:, :], t2[:, :]),
                  ins=[t1, t2], outs=[t1])
                for ci in range(C):
                    E("tensor", lambda t, ci=ci: t.transpose(
                        ptr, t1[:, ci * 128:(ci + 1) * 128], eye[:, :]),
                      ins=[t1, eye], outs=[ptr_t])
                    E("vector", lambda v: v.tensor_copy(wqb[:, :], ptr),
                      ins=[ptr_t], outs=[wqb])
                    dma_out(wqT_ap[cp * D + ci * 128: cp * D + (ci + 1) * 128,
                                   it * 128:(it + 1) * 128],
                            wqb[:, :], [wqb], wrote=[wqT_ap])

    for ci in range(C):
        dma_in(t2[:, :], fac_in[0, ci * 128:(ci + 1) * 128, :], [t2])
        for r in range(1, 8):
            dma_in(t1[:, :], fac_in[r, ci * 128:(ci + 1) * 128, :], [t1])
            E("vector", lambda v: v.tensor_add(t2[:, :], t2[:, :], t1[:, :]),
              ins=[t2, t1], outs=[t2])
        dma_out(F_hbm[ci * 128:(ci + 1) * 128, :], t2[:, :], [t2], wrote=[F_hbm])

    def attn_phase_A(li, xsrc, AR):
        G0, G1, G2, G3 = AR
        for g in range(NG):
            for tt in range(TPG):
                row0 = g * GRP + tt * TT
                dma_in(x_t[:, :], xsrc[row0:row0 + TT, :], [x_t], src=[xsrc])
                ln_stats(x_t, x_t[:, :], D)
                E("vector", lambda v: v.tensor_scalar(xn_t[:, :], x_t[:, :],
                                                      st_mu[:, :], st_r[:, :],
                                                      ALU.subtract, ALU.mult),
                  ins=[x_t, st_mu, st_r], outs=[xn_t])
                dma_out(xn_hbm[row0:row0 + TT, :], xn_t[:, :], [xn_t],
                        wrote=[xn_hbm])
                ln_stats(xn_t, xn_t[:, :], D)
                E("vector", lambda v: v.tensor_scalar(t2[:, :], xn_t[:, :],
                                                      st_mu[:, :], st_r[:, :],
                                                      ALU.subtract, ALU.mult),
                  ins=[xn_t, st_mu, st_r], outs=[t2])
                E("vector", lambda v: v.tensor_mul(t2[:, :], t2[:, :],
                                                   gam_rep[:, :]),
                  ins=[t2, gam_rep], outs=[t2])
                E("vector", lambda v: v.tensor_add(t2[:, :], t2[:, :],
                                                   bet_rep[:, :]),
                  ins=[t2, bet_rep], outs=[t2])
                for ci in range(C):
                    c0 = ci * 512 + tt * 128
                    tblock(t2, t2[:, ci * 128:(ci + 1) * 128], G0,
                           G0[:, c0:c0 + 128], dst_marks=[g0m[tt]])
            for op_i in range(DQ // 512):
                for h in range(2):
                    wsrc = Win[li][:, op_i * 512:(op_i + 1) * 512] \
                        .rearrange("(c q) o -> q c o", q=128)[:, h * 8:(h + 1) * 8, :]
                    E("sync", lambda s, h=h, w=wsrc: s.dma_start(
                        out=G3[:, h * 4096:(h + 1) * 4096]
                        .rearrange("q (c o) -> q c o", c=8), in_=w),
                      outs=[gh[h]], inc=16, is_async=True, chan_obj=gh[h])
                bcast_in(b_sl, b_sl[:, :], Bin[li][op_i * 512:(op_i + 1) * 512],
                         512)
                bs_ = (op_i % 2) * 4
                for ci in range(C):
                    for tt in range(TPG):
                        E("tensor", lambda t, ci=ci, tt=tt, bs_=bs_: t.matmul(
                            pm[bs_ + tt][:, :],
                            G0[:, ci * 512 + tt * 128: ci * 512 + tt * 128 + 128],
                            G3[:, ci * 512:(ci + 1) * 512],
                            start=(ci == 0), stop=(ci == C - 1)),
                          ins=[g0m[tt], gh[ci // 8]], outs=[pm[bs_ + tt]])
                o0 = op_i * 512
                for tt in range(TPG):
                    bank = pm[bs_ + tt]
                    row0 = g * GRP + tt * TT
                    if o0 < D:
                        E("vector", lambda v, bank=bank: v.scalar_tensor_tensor(
                            ev0[:, :], bank[:, :], 1.0, b_sl[:, :],
                            ALU.mult, ALU.add),
                          ins=[bank, b_sl], outs=[ev0])
                        dma_out(q_hbm[row0:row0 + TT, o0:o0 + 512], ev0[:, :],
                                [ev0], wrote=[q_hbm])
                    else:
                        if o0 < 2 * D:
                            c0 = tt * 2048 + (o0 - D)
                            dt_, da = G1, G1[:, c0:c0 + 512]
                        else:
                            c0 = tt * 2048 + (o0 - 2 * D)
                            dt_, da = G2, G2[:, c0:c0 + 512]
                        E("vector", lambda v, bank=bank, da=da:
                          v.scalar_tensor_tensor(
                              da, bank[:, :], 1.0, b_sl[:, :], ALU.mult,
                              ALU.add),
                          ins=[bank, b_sl], outs=[dt_])
            for tt in range(TPG):
                row0 = g * GRP + tt * TT
                kap = G1[:, tt * 2048: tt * 2048 + D]
                vap = G2[:, tt * 2048: tt * 2048 + D]
                rotary(G1, tt * 2048, row0)
                if li == 0:
                    rotary(G2, tt * 2048, row0)
                E("scalar", lambda s, a=kap: s.activation(out=t1[:, :], in_=a,
                                                          func=AF.Square),
                  ins=[G1], outs=[t1])
                red_wide(st_a, st_a[:, :], t1, t1[:, 0:1024],
                         t1[:, 1024:2048], "sum")
                E("scalar", lambda s: s.activation(out=st_a[:, :], in_=st_a[:, :],
                                                   func=AF.Sqrt),
                  ins=[st_a], outs=[st_a])
                E("vector", lambda v: v.reciprocal(st_a[:, :], st_a[:, :]),
                  ins=[st_a], outs=[st_a])
                if li == 0:
                    E("scalar", lambda s, a=vap: s.activation(out=t1[:, :], in_=a,
                                                              func=AF.Square),
                      ins=[G2], outs=[t1])
                    red_wide(st_b, st_b[:, :], t1, t1[:, 0:1024],
                             t1[:, 1024:2048], "sum")
                    E("scalar", lambda s: s.activation(out=st_b[:, :],
                                                       in_=st_b[:, :],
                                                       func=AF.Sqrt),
                      ins=[st_b], outs=[st_b])
                    E("vector", lambda v: v.reciprocal(st_b[:, :], st_b[:, :]),
                      ins=[st_b], outs=[st_b])
                    E("vector", lambda v: v.tensor_mul(st_a[:, :], st_a[:, :],
                                                       st_b[:, :]),
                      ins=[st_a, st_b], outs=[st_a])
                E("vector", lambda v, ka=kap, va=vap: v.tensor_mul(t1[:, :], ka,
                                                                   va),
                  ins=[G1, G2], outs=[t1])
                E("vector", lambda v: v.tensor_scalar_mul(t1[:, :], t1[:, :],
                                                          st_a[:, :]),
                  ins=[t1, st_a], outs=[t1])
                first = (g == 0 and tt == 0)
                for ci in range(C):
                    E("tensor", lambda t, ci=ci: t.matmul(
                        pA[:, ci:ci + 1], t1[:, ci * 128:(ci + 1) * 128],
                        ones_col[:, :], start=True, stop=True),
                      ins=[t1, ones_col], outs=[pA_t])
                if first:
                    E("vector", lambda v: v.tensor_copy(A_acc[:, :], pA),
                      ins=[pA_t], outs=[A_acc])
                else:
                    E("vector", lambda v: v.tensor_add(A_acc[:, :], A_acc[:, :],
                                                       pA),
                      ins=[pA_t], outs=[A_acc])

    def exchange_A(li):
        E("vector", lambda v: v.tensor_copy(A_loc[:, :], A_acc[:, :]),
          ins=[A_acc], outs=[A_loc])
        E("vector", lambda v: v.tensor_scalar_mul(A2[:, 0:C], A_loc[:, :],
                                                  bmask[:, 0:1]),
          ins=[A_loc, bmask], outs=[A2])
        E("vector", lambda v: v.tensor_scalar_mul(A2[:, C:2 * C], A_loc[:, :],
                                                  bmask[:, 1:2]),
          ins=[A_loc, bmask], outs=[A2])
        cc_dst = ccin[:, :].rearrange("b (c q) -> q b c", c=C, q=128)
        a2_src = A2[:, :].rearrange("q (b c) -> q b c", b=B)
        E("gpsimd", lambda gp: gp.dma_start(out=cc_dst, in_=a2_src,
                                            allow_slow_non_contiguous=True),
          ins=[A2], outs=[ccin], inc=16, is_async=True)
        p.barrier()
        E("gpsimd", lambda gp: gp.collective_compute(
            "AllReduce", ALU.add, replica_groups=[list(range(NCORES))],
            ins=[ccin[:, :]], outs=[ccout[:, :]]),
          ins=[ccin], outs=[ccout], inc=1, is_async=True, sem_eng="cc")
        p.barrier()
        dma_in(t2[0:2, :], ccout[:, :], [t2], src=[ccout])
        for j in range(4):
            E("tensor", lambda t, j=j: t.matmul(
                pR[:1, :], bsel[:, :], t2[0:2, j * 512:(j + 1) * 512],
                start=True, stop=True),
              ins=[bsel, t2], outs=[pR_t])
            E("vector", lambda v, j=j: v.tensor_copy(
                t1[0:1, j * 512:(j + 1) * 512], pR[:1, :]),
              ins=[pR_t], outs=[t1])
        if li == 0:
            E("vector", lambda v: v.tensor_scalar(t1[0:1, :], t1[0:1, :],
                                                  0.5, 0.5, ALU.mult, ALU.add),
              ins=[t1], outs=[t1])
        for j in range(4):
            E("tensor", lambda t, j=j: t.matmul(
                pR[:, :], ones_row[:, :], t1[0:1, j * 512:(j + 1) * 512],
                start=True, stop=True),
              ins=[ones_row, t1], outs=[pR_t])
            E("vector", lambda v, j=j: v.tensor_copy(
                arep[:, j * 512:(j + 1) * 512], pR[:, :]),
              ins=[pR_t], outs=[arep])
        if dbg:
            dma_out(dbgt[:, 0:C], A_loc[:, :], [A_loc], wrote=[dbgt])
            dma_out(dbgt[:, D:2 * D], arep[:, :], [arep], wrote=[dbgt])
            dma_out(dbgt[0:2, 2 * D:3 * D], t2[0:2, :], [t2], wrote=[dbgt])
            dma_out(dbgt[0:1, 3 * D:4 * D], t1[0:1, :], [t1], wrote=[dbgt])
            E("gpsimd", lambda gp: gp.dma_start(out=dbgt[2:4, 3 * D:4 * D],
                                                in_=ccin[:, :]),
              ins=[ccin], outs=[dbgt], inc=16, is_async=True)
            dma_out(dbgt[4:6, 2 * D:2 * D + 2 * C], A2[0:2, :2 * C], [A2], wrote=[dbgt])

    def attn_phase_B(li, AR):
        G0, G1, G2, G3 = AR
        for g in range(NG):
            for tt in range(TPG):
                row0 = g * GRP + tt * TT
                dma_in(q_t[:, :], q_hbm[row0:row0 + TT, :], [q_t], src=[q_hbm])
                if li == 1:
                    rotary(q_t, 0, row0)
                E("vector", lambda v: v.tensor_mul(q_t[:, :], q_t[:, :],
                                                   arep[:, :]),
                  ins=[q_t, arep], outs=[q_t])
                for ci in range(C):
                    c0 = ci * 512 + tt * 128
                    tblock(q_t, q_t[:, ci * 128:(ci + 1) * 128], G0,
                           G0[:, c0:c0 + 128], dst_marks=[g0m[tt]])
            for op_i in range(D // 512):
                for h in range(2):
                    wsrc = Wout[li][:, op_i * 512:(op_i + 1) * 512] \
                        .rearrange("(c q) o -> q c o", q=128)[:, h * 8:(h + 1) * 8, :]
                    E("sync", lambda s, h=h, w=wsrc: s.dma_start(
                        out=G3[:, h * 4096:(h + 1) * 4096]
                        .rearrange("q (c o) -> q c o", c=8), in_=w),
                      outs=[gh[h]], inc=16, is_async=True, chan_obj=gh[h])
                bcast_in(b_sl, b_sl[:, :], Bout[li][op_i * 512:(op_i + 1) * 512],
                         512)
                bs_ = (op_i % 2) * 4
                for ci in range(C):
                    for tt in range(TPG):
                        E("tensor", lambda t, ci=ci, tt=tt, bs_=bs_: t.matmul(
                            pm[bs_ + tt][:, :],
                            G0[:, ci * 512 + tt * 128: ci * 512 + tt * 128 + 128],
                            G3[:, ci * 512:(ci + 1) * 512],
                            start=(ci == 0), stop=(ci == C - 1)),
                          ins=[g0m[tt], gh[ci // 8]], outs=[pm[bs_ + tt]])
                for tt in range(TPG):
                    bank = pm[bs_ + tt]
                    row0 = g * GRP + tt * TT
                    osl = slice(op_i * 512, (op_i + 1) * 512)
                    dma_in(ev1[:, :], xn_hbm[row0:row0 + TT, osl], [ev1],
                           src=[xn_hbm])
                    E("vector", lambda v, bank=bank: v.scalar_tensor_tensor(
                        ev0[:, :], bank[:, :], 1.0, b_sl[:, :], ALU.mult,
                        ALU.add),
                      ins=[bank, b_sl], outs=[ev0])
                    E("vector", lambda v: v.tensor_add(ev0[:, :], ev0[:, :],
                                                       ev1[:, :]),
                      ins=[ev0, ev1], outs=[ev0])
                    dma_out(s_hbm[row0:row0 + TT, osl], ev0[:, :], [ev0],
                            wrote=[s_hbm])

    def ffn0_part1(kqT, wgu):
        kqm = [object() for _ in range(NTT)]
        E("vector", lambda v: v.memset(hm_all[:, :], 0.0), outs=[hm_all])
        for tt in range(NTT):
            row0 = tt * TT
            dma_in(t2[:, :], xn_hbm[row0:row0 + TT, :], [t2], src=[xn_hbm])
            red_wide(st_a, st_a[:, :], t2, t2[:, 0:1024], t2[:, 1024:2048],
                     "max", absval=True)
            E("vector", lambda v: v.tensor_scalar_max(st_a[:, :], st_a[:, :],
                                                      QEPS),
              ins=[st_a], outs=[st_a])
            E("vector", lambda v: v.reciprocal(st_r[:, :], st_a[:, :]),
              ins=[st_a], outs=[st_r])
            E("vector", lambda v: v.tensor_scalar(t2[:, :], t2[:, :],
                                                  st_r[:, :], 127.0,
                                                  ALU.mult, ALU.mult),
              ins=[t2, st_r], outs=[t2])
            round_clip(t2, t2[:, 0:D // 2], D // 2, -128.0, 127.0)
            round_clip(t2, t2[:, D // 2:D], D // 2, -128.0, 127.0)
            for ci in range(C):
                E("tensor", lambda t, ci=ci: t.transpose(
                    ptr, t2[:, ci * 128:(ci + 1) * 128], eye[:, :]),
                  ins=[t2, eye], outs=[ptr_t])
                c0 = ci * 2048 + tt * 128
                E("vector", lambda v, c0=c0: v.tensor_copy(kqT[:, c0:c0 + 128],
                                                           ptr),
                  ins=[ptr_t], outs=[kqm[tt]])
            E("vector", lambda v: v.tensor_scalar_mul(st_a[:, :], st_a[:, :],
                                                      1.0 / 127.0),
              ins=[st_a], outs=[st_a])
            E("vector", lambda v, tt=tt: v.tensor_tensor(rs_g[:, tt:tt + 1],
                                                         st_a[:, :],
                                                         wsc[:, 0:1], ALU.mult),
              ins=[st_a, wsc], outs=[rs_g])
            E("vector", lambda v, tt=tt: v.tensor_tensor(rs_u[:, tt:tt + 1],
                                                         st_a[:, :],
                                                         wsc[:, 1:2], ALU.mult),
              ins=[st_a, wsc], outs=[rs_u])
        for op_i in range(DF // 512):
            osl = slice(op_i * 512, (op_i + 1) * 512)
            E("sync", lambda s, osl=osl: s.dma_start(
                out=wgu[:, 0:8192].rearrange("q (c o) -> q c o", c=C),
                in_=wqgT_h[:, osl].rearrange("(c q) o -> q c o", q=128)),
              ins=[wqgT_h], outs=[wgu], inc=16, is_async=True, chan_obj=gh[0])
            E("sync", lambda s, osl=osl: s.dma_start(
                out=wgu[:, 8192:16384].rearrange("q (c o) -> q c o", c=C),
                in_=wquT_h[:, osl].rearrange("(c q) o -> q c o", q=128)),
              ins=[wquT_h], outs=[wgu], inc=16, is_async=True, chan_obj=gh[1])
            for tt in range(NTT):
                bg, bu = pm[2 * (tt % 4)], pm[2 * (tt % 4) + 1]
                for ci in range(C):
                    c0 = ci * 2048 + tt * 128
                    E("tensor", lambda t, ci=ci, c0=c0, bg=bg, tt=tt: t.matmul(
                        bg[:, :], kqT[:, c0:c0 + 128],
                        wgu[:, ci * 512:(ci + 1) * 512],
                        start=(ci == 0), stop=(ci == C - 1)),
                      ins=[kqm[tt], wgu], outs=[bg])
                    E("tensor", lambda t, ci=ci, c0=c0, bu=bu, tt=tt: t.matmul(
                        bu[:, :], kqT[:, c0:c0 + 128],
                        wgu[:, 8192 + ci * 512: 8192 + (ci + 1) * 512],
                        start=(ci == 0), stop=(ci == C - 1)),
                      ins=[kqm[tt], wgu], outs=[bu])
                E("scalar", lambda s, bg=bg, tt=tt: s.activation(
                    out=ev0[:, :], in_=bg[:, :], func=AF.Silu,
                    scale=rs_g[:, tt:tt + 1]),
                  ins=[bg, rs_g], outs=[ev0])
                E("vector", lambda v, bu=bu: v.tensor_mul(ev1[:, :], ev0[:, :],
                                                          bu[:, :]),
                  ins=[ev0, bu], outs=[ev1])
                E("vector", lambda v, tt=tt: v.tensor_scalar_mul(
                    ev1[:, :], ev1[:, :], rs_u[:, tt:tt + 1]),
                  ins=[ev1, rs_u], outs=[ev1])
                E("vector", lambda v: v.reduce_max(st_b[:, :], ev1[:, :],
                                                   axis=AXX,
                                                   apply_absolute_value=True),
                  ins=[ev1], outs=[st_b])
                E("vector", lambda v, tt=tt: v.tensor_max(hm_all[:, tt:tt + 1],
                                                          hm_all[:, tt:tt + 1],
                                                          st_b[:, :]),
                  ins=[st_b, hm_all], outs=[hm_all])
                dma_out(h_hbm[tt * TT:(tt + 1) * TT, osl], ev1[:, :], [ev1],
                        wrote=[h_hbm])
        for tt in range(NTT):
            E("vector", lambda v, tt=tt: v.tensor_scalar_max(
                st_a[:, :], hm_all[:, tt:tt + 1], QEPS),
              ins=[hm_all], outs=[st_a])
            E("vector", lambda v: v.reciprocal(st_r[:, :], st_a[:, :]),
              ins=[st_a], outs=[st_r])
            E("vector", lambda v, tt=tt: v.tensor_scalar_mul(
                rs_d[:, tt:tt + 1], st_r[:, :], 127.0),
              ins=[st_r], outs=[rs_d])
            E("vector", lambda v: v.tensor_scalar_mul(st_a[:, :], st_a[:, :],
                                                      1.0 / 127.0),
              ins=[st_a], outs=[st_a])
            E("vector", lambda v, tt=tt: v.tensor_tensor(rs_o[:, tt:tt + 1],
                                                         st_a[:, :],
                                                         wsc[:, 2:3], ALU.mult),
              ins=[st_a, wsc], outs=[rs_o])

    def ffn0_part2(k2T):
        k2m = [object() for _ in range(8)]

        def quant_tile(hf, t8):
                tt = hf * 8 + t8
                row0 = tt * TT
                for fp_i in range(DF // 512):
                    dma_in(ev0[:, :], h_hbm[row0:row0 + TT,
                                            fp_i * 512:(fp_i + 1) * 512], [ev0],
                           src=[h_hbm])
                    E("vector", lambda v, tt=tt: v.tensor_scalar_mul(
                        t2[:, :512], ev0[:, :], rs_d[:, tt:tt + 1]),
                      ins=[ev0, rs_d], outs=[t2])
                    round_clip(t2, t2[:, :512], 512, -128.0, 127.0)
                    for j in range(4):
                        fc = fp_i * 4 + j
                        c0 = fc * 1024 + t8 * 128
                        E("tensor", lambda t, j=j: t.transpose(
                            ptr, t2[:, j * 128:(j + 1) * 128], eye[:, :]),
                          ins=[t2, eye], outs=[ptr_t])
                        E("vector", lambda v, c0=c0, t8=t8: v.tensor_copy(
                            k2T[:, c0:c0 + 128], ptr),
                          ins=[ptr_t], outs=[k2m[t8]])

        def down_quad(hf, q4):
                for op_i in range(D // 512):
                    osl = slice(op_i * 512, (op_i + 1) * 512)
                    for fc in range(DF // 128):
                        wb = (wdt, wdt) if False else ((wdt, wdt2)[fc % 2],) * 1
                        wcur = (wdt, wdt2)[fc % 2]
                        dma_in(wcur[:, :], wqdT_h[fc * 128:(fc + 1) * 128, osl],
                               [wcur], src=[wqdT_h])
                        for k in range(4):
                            t8 = q4 * 4 + k
                            c0 = fc * 1024 + t8 * 128
                            bs_ = ((q4 * 4 + op_i) % 2) * 4
                            E("tensor", lambda t, c0=c0, k=k, fc=fc, wcur=wcur,
                              t8=t8, bs_=bs_: t.matmul(
                                  pm[bs_ + k][:, :], k2T[:, c0:c0 + 128],
                                  wcur[:, :],
                                  start=(fc == 0), stop=(fc == DF // 128 - 1)),
                              ins=[k2m[t8], wcur], outs=[pm[bs_ + k]])
                    for k in range(4):
                        t8 = q4 * 4 + k
                        tt = hf * 8 + t8
                        row0 = tt * TT
                        bs_ = ((q4 * 4 + op_i) % 2) * 4
                        E("scalar", lambda s, k=k, tt=tt, bs_=bs_: s.activation(
                            out=ev0[:, :], in_=pm[bs_ + k][:, :], func=AF.Silu,
                            scale=rs_o[:, tt:tt + 1]),
                          ins=[pm[bs_ + k], rs_o], outs=[ev0])
                        osl2 = slice(op_i * 512, (op_i + 1) * 512)
                        dma_in(ev1[:, :], s_hbm[row0:row0 + TT, osl2], [ev1],
                               src=[s_hbm])
                        dma_in(b_sl[:, :], x_in[row0:row0 + TT, osl2], [b_sl])
                        E("vector", lambda v: v.tensor_mul(ev1[:, :], ev1[:, :],
                                                           ev0[:, :]),
                          ins=[ev0, ev1], outs=[ev1])
                        E("vector", lambda v: v.tensor_add(ev1[:, :], ev1[:, :],
                                                           b_sl[:, :]),
                          ins=[ev1, b_sl], outs=[ev1])
                        dma_out(x1_hbm[row0:row0 + TT, osl2], ev1[:, :], [ev1],
                                wrote=[x1_hbm])

        for hf in range(2):
            for t8 in range(4):
                quant_tile(hf, t8)
            down_quad(hf, 0)
            for t8 in range(4, 8):
                quant_tile(hf, t8)
            down_quad(hf, 1)

    def ffn1(AR):
        G0, G1, G2, G3 = AR
        bcast_in(bet_rep, bet_rep[:, :], fb_in, D)
        for g in range(NG):
            for tt in range(TPG):
                row0 = g * GRP + tt * TT
                dma_in(xn_t[:, :], xn_hbm[row0:row0 + TT, :], [xn_t],
                       src=[xn_hbm])
                for ci in range(C):
                    c0 = ci * 512 + tt * 128
                    tblock(xn_t, xn_t[:, ci * 128:(ci + 1) * 128],
                           G0, G0[:, c0:c0 + 128], dst_marks=[g0m[tt]])
            for op_i in range(D // 512):
                osl = slice(op_i * 512, (op_i + 1) * 512)
                for h in range(2):
                    wsrc = F_hbm[:, osl] \
                        .rearrange("(c q) o -> q c o", q=128)[:, h * 8:(h + 1) * 8, :]
                    E("sync", lambda s, h=h, w=wsrc: s.dma_start(
                        out=G3[:, h * 4096:(h + 1) * 4096]
                        .rearrange("q (c o) -> q c o", c=8), in_=w),
                      ins=[F_hbm], outs=[gh[h]], inc=16, is_async=True,
                      chan_obj=gh[h])
                bs_ = (op_i % 2) * 4
                for ci in range(C):
                    for tt in range(TPG):
                        E("tensor", lambda t, ci=ci, tt=tt, bs_=bs_: t.matmul(
                            pm[bs_ + tt][:, :],
                            G0[:, ci * 512 + tt * 128: ci * 512 + tt * 128 + 128],
                            G3[:, ci * 512:(ci + 1) * 512],
                            start=(ci == 0), stop=(ci == C - 1)),
                          ins=[g0m[tt], gh[ci // 8]], outs=[pm[bs_ + tt]])
                for tt in range(TPG):
                    bank = pm[bs_ + tt]
                    row0 = g * GRP + tt * TT
                    E("vector", lambda v, bank=bank, osl=osl:
                      v.scalar_tensor_tensor(
                          ev0[:, :], bank[:, :], RSQRT8, bet_rep[:, osl],
                          ALU.mult, ALU.add),
                      ins=[bank, bet_rep], outs=[ev0])
                    E("scalar", lambda s: s.activation(out=ev0[:, :],
                                                       in_=ev0[:, :],
                                                       func=AF.Silu),
                      ins=[ev0], outs=[ev0])
                    dma_in(ev1[:, :], s_hbm[row0:row0 + TT, osl], [ev1],
                           src=[s_hbm])
                    dma_in(b_sl[:, :], x1_hbm[row0:row0 + TT, osl], [b_sl],
                           src=[x1_hbm])
                    E("vector", lambda v: v.tensor_mul(ev1[:, :], ev1[:, :],
                                                       ev0[:, :]),
                      ins=[ev0, ev1], outs=[ev1])
                    E("vector", lambda v: v.tensor_add(ev1[:, :], ev1[:, :],
                                                       b_sl[:, :]),
                      ins=[ev1, b_sl], outs=[ev1])
                    dma_out(y_out[row0:row0 + TT, osl], ev1[:, :], [ev1],
                            wrote=[y_out])

    def done(tag):
        return upto is not None and upto == tag

    ar_es = ExitStack()
    AR = tuple(sbt(ar_es, "AR%d" % i, [128, 8192]) for i in range(4))
    bcast_in(gam_rep, gam_rep[:, :], Gam[0], D)
    bcast_in(bet_rep, bet_rep[:, :], Bet[0], D)
    attn_phase_A(0, x_in, AR)
    if not done("L0A"):
        if not done("noX"):
            exchange_A(0)
        if not done("L0X"):
            attn_phase_B(0, AR)
    p.barrier()
    ar_es.close()

    if not done("L0A") and not done("noX") and not done("L0X")             and not done("L0B"):
        ar_es = ExitStack()
        kqT = sbt(ar_es, "kqT", [128, 2 * 16384], BF16)
        wgu = sbt(ar_es, "wgu", [128, 16384], BF16)
        ffn0_part1(kqT, wgu)
        p.barrier()
        ar_es.close()

        if not done("F01"):
            ar_es = ExitStack()
            k2T = sbt(ar_es, "k2T", [128, 64 * 1024], BF16)
            ffn0_part2(k2T)
            p.barrier()
            ar_es.close()

            if not done("F02"):
                ar_es = ExitStack()
                AR = tuple(sbt(ar_es, "BR%d" % i, [128, 8192])
                           for i in range(4))
                bcast_in(gam_rep, gam_rep[:, :], Gam[1], D)
                bcast_in(bet_rep, bet_rep[:, :], Bet[1], D)
                attn_phase_A(1, x1_hbm, AR)
                exchange_A(1)
                attn_phase_B(1, AR)
                ffn1(AR)
                p.barrier()
                ar_es.close()

    with nc.Block() as block:
        p.build(nc, block, es)
    pes.close()
    es.close()
    return nc


def _rot_tables():
    try:
        import jax
        import jax.numpy as jnp
        cpu = jax.devices("cpu")[0]
        with jax.default_device(cpu):
            inv = 1.0 / 10000.0 ** (jnp.arange(0, ROT, 2, dtype=jnp.float32) / ROT)
            fr = jnp.arange(S, dtype=jnp.float32)[:, None] * inv[None, :]
            fr = jnp.repeat(fr, 2, axis=-1)
            cost = np.asarray(jnp.cos(fr), dtype=np.float32)
            sint = np.asarray(jnp.sin(fr), dtype=np.float32)
    except Exception:
        inv = (1.0 / 10000.0 ** (np.arange(0, ROT, 2, dtype=np.float32)
                                 / np.float32(ROT))).astype(np.float32)
        fr = (np.arange(S, dtype=np.float32)[:, None]
              * inv[None, :]).astype(np.float32)
        fr = np.repeat(fr, 2, axis=-1)
        cost, sint = np.cos(fr).astype(np.float32), np.sin(fr).astype(np.float32)
    return cost, sint


def kernel(L0_attn_X, L0_attn_W_in, L0_attn_b_in, L0_attn_W_out, L0_attn_b_out,
           L0_attn_gamma, L0_attn_beta, L0_ffn_w_gate, L0_ffn_w_up,
           L0_ffn_w_down, L1_attn_ln_w, L1_attn_ln_b, L1_attn_W_in,
           L1_attn_b_in, L1_attn_W_out, L1_attn_b_out, L1_ffn_factors,
           L1_ffn_bias):
    x = np.ascontiguousarray(np.asarray(L0_attn_X, dtype=np.float32))
    cost, sint = _rot_tables()
    cs_full = np.concatenate([cost, sint], axis=1)

    com = {
        "eye_w": np.eye(128, dtype=np.float32),
        "w_in0": np.ascontiguousarray(L0_attn_W_in, dtype=np.float32),
        "b_in0": np.ascontiguousarray(L0_attn_b_in, dtype=np.float32),
        "w_out0": np.ascontiguousarray(L0_attn_W_out, dtype=np.float32),
        "b_out0": np.ascontiguousarray(L0_attn_b_out, dtype=np.float32),
        "gam0": np.ascontiguousarray(L0_attn_gamma, dtype=np.float32),
        "bet0": np.ascontiguousarray(L0_attn_beta, dtype=np.float32),
        "wg": np.ascontiguousarray(L0_ffn_w_gate, dtype=np.float32),
        "wu": np.ascontiguousarray(L0_ffn_w_up, dtype=np.float32),
        "wd": np.ascontiguousarray(L0_ffn_w_down, dtype=np.float32),
        "gam1": np.ascontiguousarray(L1_attn_ln_w, dtype=np.float32),
        "bet1": np.ascontiguousarray(L1_attn_ln_b, dtype=np.float32),
        "w_in1": np.ascontiguousarray(L1_attn_W_in, dtype=np.float32),
        "b_in1": np.ascontiguousarray(L1_attn_b_in, dtype=np.float32),
        "w_out1": np.ascontiguousarray(L1_attn_W_out, dtype=np.float32),
        "b_out1": np.ascontiguousarray(L1_attn_b_out, dtype=np.float32),
        "facs": np.ascontiguousarray(L1_ffn_factors, dtype=np.float32),
        "fbias": np.ascontiguousarray(L1_ffn_bias, dtype=np.float32),
    }

    in_maps = []
    for c in range(NCORES):
        b = c // 4
        r0 = (c % 4) * NT
        bm = np.zeros((128, 2), np.float32)
        bm[:, b] = 1.0
        bs = np.zeros((2, 1), np.float32)
        bs[b, 0] = 1.0
        m = dict(com)
        m["x"] = np.ascontiguousarray(x[b, r0:r0 + NT, :])
        m["cs"] = np.ascontiguousarray(cs_full[r0:r0 + NT, :])
        m["bmask_w"] = bm
        m["bsel_w"] = bs
        in_maps.append(m)

    nc = build_nc()
    res = run_bass_kernel_spmd(nc, in_maps, core_ids=list(range(NCORES)))
    out = np.empty((B, S, D), np.float32)
    for c in range(NCORES):
        b = c // 4
        r0 = (c % 4) * NT
        out[b, r0:r0 + NT, :] = res.results[c]["y"]
    return out


if __name__ == "__main__":
    print("building program ...")
    nc = build_nc()
    n = sum(1 for _ in nc.all_instructions())
    print("instructions:", n)
